# version 15
# baseline (speedup 1.0000x reference)
import contextlib
import numpy as np
import ml_dtypes
import concourse.bass as bass
import concourse.mybir as mybir
from concourse.bass_utils import run_bass_kernel_spmd

F32 = mybir.dt.float32
BF16 = mybir.dt.bfloat16
AF = mybir.ActivationFunctionType
ALU = mybir.AluOpType
AX = mybir.AxisListType

B, T, C = 4, 4096, 1024
H, KD, VD, VBD = 16, 64, 128, 2048
FD = 64
E = 8
OWN = 4
EPS_RET = float(np.finfo(np.float32).eps)
LN_EPS = 1e-5
RMS_EPS = 1e-6
SM_SCALE = FD ** -0.5

_PHASES = 5
_B_PAIRS = 8
_B_SACC = True
_B_STAGE = 9
_B_S2 = 9
_DBG = False


def _gperm(half):
    return [p ^ 1 if half else p for p in range(8)]

def _host_tables(half):
    G = _gperm(half)
    gam = np.log1p(-np.power(2.0, -5.0 - np.arange(H, dtype=np.float64)))
    t64 = np.arange(T, dtype=np.float64)
    r = np.exp(gam)[:, None]
    scale = np.sqrt((1.0 - r ** (t64[None, :] + 1.0)) / (1.0 - r))
    qsc = np.zeros((H, OWN, 512), np.float64)
    for j in range(OWN):
        t0 = 512 * G[2 * j]
        qsc[:, j, :] = (np.exp(gam[:, None] * np.arange(512)[None, :])
                        / scale[:, t0:t0 + 512])
    ksc = np.exp(-gam[:, None] * (np.arange(512)[None, :] % 128))
    Ct = np.zeros((H, OWN, 32), np.float64)
    for j in range(OWN):
        t0 = 512 * G[2 * j]
        for c in range(8 * (j + 1)):
            s0 = 512 * G[c // 4] + 128 * (c % 4)
            with np.errstate(under='ignore'):
                Ct[:, j, c] = np.exp(gam * (t0 - s0))
    Ctab = np.tile(Ct.reshape(1, -1).astype(np.float32), (128, 1))
    mask8 = np.zeros((OWN, 8, 128, 512), np.float32)
    sr = np.arange(128)[:, None]
    tr = np.arange(512)[None, :]
    for j in range(OWN):
        t0 = 512 * G[2 * j]
        for lc in range(8):
            c = 8 * j + lc
            s0 = 512 * G[c // 4] + 128 * (c % 4)
            mask8[j, lc] = ((t0 + tr) >= (s0 + sr)).astype(np.float32)
    inv = 1.0 / 10000.0 ** (np.arange(0, FD, 2, dtype=np.float64) / FD)
    invf = np.concatenate([inv, inv])
    sq = np.sqrt(SM_SCALE)
    tq = np.concatenate([512 * G[2 * j] + np.arange(512) for j in range(OWN)])
    tk = np.concatenate([512 * G[p] + np.arange(512) for p in range(8)])
    bf = ml_dtypes.bfloat16
    return dict(
        qscd=qsc.reshape(H * OWN, 512).astype(bf),
        kscd=ksc.astype(bf),
        Ctab=np.ascontiguousarray(Ctab),
        mask8=mask8.astype(bf),
        rkc=_dup(np.cos(invf[:, None] * tk[None, :]) * sq).astype(bf),
        rks=_dup(np.sin(invf[:, None] * tk[None, :]) * sq).astype(bf),
    )

def _dup(t):
    return np.concatenate([t, t], axis=0)

def _consts():
    bf = ml_dtypes.bfloat16
    oh = np.zeros((8, 1024), np.float32)
    for e in range(8):
        oh[e, 128 * e:128 * e + 128] = 1.0
    return dict(
        fconst=np.tile(np.array([[LN_EPS, RMS_EPS, 0.0, 0.0]], np.float32), (128, 1)),
        idf=np.eye(128, dtype=np.float32),
        idb=np.eye(128, dtype=np.float32).astype(bf),
        onescol_bf=np.ones((128, 1), np.float32).astype(bf),
        ones1_bf=np.ones((1, 128), np.float32).astype(bf),
        ones1_f=np.ones((1, 128), np.float32),
        onehots=oh.astype(bf),
    )


class Prog:
    ENGS = ('pe', 'act', 'dve', 'sp', 'io')

    def __init__(self):
        self.ops = {e: [] for e in self.ENGS}
        self.w = {}
        self.r = {}
        self.dma_cnt = {}

    @staticmethod
    def _ov(a, b):
        return not (a[1] <= b[0] or b[1] <= a[0] or a[3] <= b[2] or b[3] <= a[2])

    def _deps(self, reads, writes, e=None):
        deps = []
        for (t, rect) in reads:
            for (r2, n) in self.w.get(t, ()):
                if self._ov(rect, r2):
                    deps.append(n)
            if t.startswith('ps') and e is not None:
                for (r2, n) in self.r.get(t, ()):
                    if n[0] != 'd' and n[0] != e and self._ov(rect, r2):
                        deps.append(n)
        for (t, rect) in writes:
            for (r2, n) in self.w.get(t, ()):
                if self._ov(rect, r2):
                    deps.append(n)
            for (r2, n) in self.r.get(t, ()):
                if self._ov(rect, r2):
                    deps.append(n)
        return deps

    def _record(self, reads, writes, node):
        for (t, rect) in reads:
            self.r.setdefault(t, []).append((rect, node))
        for (t, rect) in writes:
            def keep(entry):
                r2 = entry[0]
                return not (r2[0] >= rect[0] and r2[1] <= rect[1]
                            and r2[2] >= rect[2] and r2[3] <= rect[3])
            lw = self.w.setdefault(t, [])
            lw[:] = [en for en in lw if keep(en)]
            lw.append((rect, node))
            lr = self.r.get(t)
            if lr:
                lr[:] = [en for en in lr if keep(en)]

    def _norm(self, e, idx, deps):
        be, bd = {}, {}
        for d in deps:
            if d[0] == 'd':
                bd[d[1]] = self.dma_cnt[d[1]]
            else:
                de, di = d
                if de == 'pe' and e == 'pe':
                    continue
                if de == e and di >= idx:
                    raise RuntimeError('self-dep forward')
                if be.get(de, -1) < di:
                    be[de] = di
        return be, bd

    def op(self, e, emit, reads=(), writes=(), tag=''):
        idx = len(self.ops[e])
        be, bd = self._norm(e, idx, self._deps(reads, writes, e))
        if 'cst' in self.dma_cnt:
            bd.setdefault('cst', self.dma_cnt['cst'])
        self.ops[e].append(dict(emit=emit, be=be, bd=bd, inc=False,
                                dma=None, tag=tag))
        self._record(reads, writes, (e, idx))
        return (e, idx)

    def dma(self, q, emit, sem, reads=(), writes=(), tag=''):
        idx = len(self.ops[q])
        be, bd = self._norm(q, idx, self._deps(reads, writes))
        self.dma_cnt[sem] = self.dma_cnt.get(sem, 0) + 1
        self.ops[q].append(dict(emit=emit, be=be, bd=bd, inc=False,
                                dma=sem, tag=tag))
        self._record(reads, writes, ('d', sem))

    def finalize(self, nc, stack):
        for e in self.ENGS:
            for o in self.ops[e]:
                for de, di in o['be'].items():
                    self.ops[de][di]['inc'] = True
        pref = {}
        for e in self.ENGS:
            c = 0
            pl = []
            for o in self.ops[e]:
                if o['inc']:
                    c += 1
                pl.append(c)
            pref[e] = pl
        esem = {e: stack.enter_context(nc.semaphore(f'es_{e}'))
                for e in self.ENGS if any(o['inc'] for o in self.ops[e])}
        dsem = {s: stack.enter_context(nc.semaphore(f'ds_{s}'))
                for s in self.dma_cnt}
        block = stack.enter_context(nc.Block())

        def mk(e):
            ops = self.ops[e]
            def body(eng):
                water = {}
                for o in ops:
                    for de in sorted(o['be']):
                        val = pref[de][o['be'][de]]
                        if water.get(('e', de), 0) < val:
                            eng.wait_ge(esem[de], val)
                            water[('e', de)] = val
                    for s in sorted(o['bd']):
                        val = 16 * o['bd'][s]
                        if water.get(('d', s), 0) < val:
                            eng.wait_ge(dsem[s], val)
                            water[('d', s)] = val
                    inst = o['emit'](eng)
                    if o['dma'] is not None:
                        inst.then_inc(dsem[o['dma']], 16)
                    elif o['inc']:
                        inst.then_inc(esem[e], 1)
                if e == 'act' and 'out' in dsem:
                    eng.wait_ge(dsem['out'], 16 * self.dma_cnt['out'])
            return body

        if self.ops['pe']:
            block.tensor(mk('pe'))
        if self.ops['act']:
            block.scalar(mk('act'))
        if self.ops['dve']:
            block.vector(mk('dve'))
        if self.ops['sp']:
            block.sync(mk('sp'))
        if self.ops['io']:
            block.gpsimd(mk('io'))


def RD(t, p0, p1, f0, f1):
    return (t, (p0, p1, f0, f1))

ALLR = (0, 128, 0, 1 << 30)


def _build(nc):
    din = lambda n, s, dt=F32: nc.dram_tensor(n, s, dt, kind="ExternalInput").ap()
    xp = din('xp', [T, C])
    wqkp = din('wqkp', [8, 128, 8, 256]); wvp = din('wvp', [8, 128, 8, 256])
    wgp = din('wgp', [8, 128, 8, 256]); wop = din('wop', [8, 128, 2, 1024])
    bqkq = din('bqkq', [128, 8]); bqkk = din('bqkk', [128, 8])
    bvrow = din('bvrow', [VBD]); bg = din('bg', [128, H]); bo = din('bo', [128, 8])
    Ctab_d = din('Ctab', [128, 2048])
    qscd = din('qscd', [H * OWN, 512], BF16); kscd = din('kscd', [H, 512], BF16)
    mask8_d = din('mask8', [OWN, 8, 128, 512], BF16)
    fqp = din('fqp', [4, 128, 4, 8, 256]); fop = din('fop', [4, 128, 2, 1024])
    fksp = din('fksp', [4, 128, 8, 256])
    fqb = din('fqb', [128, 8]); fkb = din('fkb', [128, 8])
    fqbs = din('fqbs', [128, 8]); fkbs = din('fkbs', [128, 8])
    fvb = din('fvb', [C]); fob = din('fob', [128, 8])
    n1w = din('n1w', [128, 8]); n2w = din('n2w', [128, 8])
    rkc = din('rkc', [128, T], BF16); rks = din('rks', [128, T], BF16)
    gwT = din('gwT', [C, E])
    w1p = din('w1p', [E, 16, 128, 8, 128]); w2p = din('w2p', [E, 16, 128, 8, 128])
    w3T = din('w3T', [E, 2048, C])
    b1l = din('b1l', [E, 128, 16]); b2l = din('b2l', [E, 128, 16])
    b3l = din('b3l', [E, 128, 8])
    idf = din('idf', [128, 128]); idb = din('idb', [128, 128], BF16)
    fconst = din('fconst', [128, 4])
    onescol = din('onescol_bf', [128, 1], BF16)
    ones1b = din('ones1_bf', [1, 128], BF16)
    ones1f = din('ones1_f', [1, 128])
    onehots_d = din('onehots', [8, 1024], BF16)

    out_d = nc.dram_tensor('out', [2048, C], F32, kind="ExternalOutput").ap()
    skind = "ExternalOutput" if _DBG else "Internal"
    xnr1d = nc.dram_tensor('xnr1d', [C, T], BF16, kind=skind).ap()
    resd = nc.dram_tensor('resd', [C, 2048], F32, kind=skind).ap()
    sd = nc.dram_tensor('sd', [C, 2048], F32, kind=skind).ap()
    w3bf = nc.dram_tensor('w3bf', [E * 2048, C], BF16, kind="Internal").ap()

    st = contextlib.ExitStack()
    P = Prog()
    with st:
        def sb(name, shape, dt):
            return st.enter_context(nc.sbuf_tensor(name, shape, dt))
        AB = [sb(f'ab{i}', [128, T], BF16) for i in range(8)]
        ZAR = sb('zar', [128, 16384], BF16)
        VAR = sb('var', [128, 8320], BF16)
        WAR = sb('war', [128, 8192], BF16)
        FAR = sb('far', [128, 4096], F32)
        SCR = sb('scr', [128, 2048], F32)
        QAR = sb('qar', [128, 2048], BF16)
        KAR = sb('kar', [128, 4096], BF16)
        MAT = sb('mat', [128, 4096], BF16)
        idf_s = sb('idf_s', [128, 128], F32)
        idb_s = sb('idb_s', [128, 128], BF16)
        onescol_s = sb('onescol_s', [128, 1], BF16)
        ones1b_s = sb('ones1b_s', [1, 128], BF16)
        ones1f_s = sb('ones1f_s', [1, 128], F32)
        onehots_s = sb('onehots_s', [8, 1024], BF16)
        Ctab_s = sb('Ctab_s', [128, 2048], F32)
        KAR2 = Ctab_s.bitcast(BF16)
        bqkq_s = sb('bqkq_s', [128, 8], F32)
        bqkk_s = sb('bqkk_s', [128, 8], F32)
        bvp = sb('bvp', [1, 256], BF16)
        bg_s = sb('bg_s', [128, H], F32)
        bo_s = sb('bo_s', [128, 8], F32)
        fqb_s = sb('fqb_s', [128, 8], F32)
        fkb_s = sb('fkb_s', [128, 8], F32)
        fqbs_s = sb('fqbs_s', [128, 8], F32)
        fkbs_s = sb('fkbs_s', [128, 8], F32)

        fob_s = sb('fob_s', [128, 8], F32)
        fc_s = sb('fc_s', [128, 4], F32)
        n1w_s = sb('n1w_s', [128, 8], F32)
        n2w_s = sb('n2w_s', [128, 8], F32)
        cols = sb('cols', [128, 16], F32)
        rows = [sb(f'row{i}', [1, 512], F32) for i in range(4)]
        krow = sb('krow', [1, 512], BF16)
        qrow = krow
        ksct = sb('ksct', [128, 512], BF16)
        qsct = sb('qsct', [128, 512], BF16)
        BT2 = sb('bt2c', [128, 2048], BF16)
        bt = [BT2[:, 512 * i:512 * i + 512] for i in range(4)] \
            + [sb('bt4', [128, 512], BF16)]
        bt5 = sb('bt5', [128, 512], BF16)
        BTR = lambda i: RD('bt2c', 0, 128, 512 * i, 512 * i + 512)
        cpT = sb('cpT', [8, 2048], BF16)
        lgt = sb('lgt', [128, 8], F32)
        m1c = sb('m1c', [128, 1], F32)
        m2c = sb('m2c', [128, 1], F32)
        i1c = sb('i1c', [128, 8], F32)
        i2c = sb('i2c', [128, 8], F32)
        t8c = sb('t8c', [128, 8], F32)
        combc = sb('combc', [128, 8], F32)
        PSA = st.enter_context(nc.psum_tensor('psa', [128, 2048], F32))
        PSV = st.enter_context(nc.psum_tensor('psv', [128, 1024], F32))
        PSP = st.enter_context(nc.psum_tensor('psp', [128, 1024], F32))
        ps = [PSA[:, 512 * i:512 * i + 512] for i in range(4)] \
            + [PSP[:, 0:512], PSP[:, 512:1024]]
        ps7 = PSV[:, 0:512]
        _PSN = ['psa', 'psa', 'psa', 'psa', 'psp', 'psp']
        _PSO = [0, 512, 1024, 1536, 0, 512]
        def PSR(i):
            if i == 7:
                return RD('psv', 0, 128, 0, 512)
            return RD(_PSN[i], 0, 128, _PSO[i], _PSO[i] + 512)
        PS7R = PSR(7)
        CST = [RD('cst', 0, 1, 0, 1)]

        def cload(dst, src, cast=False):
            P.dma('io' if cast else 'sp',
                  lambda eng, d=dst, s=src: eng.dma_start(out=d, in_=s),
                  'cst', writes=[RD('cst', 0, 1, 0, 1)], tag='cst')
        for dst, src, cast in (
                (idf_s[:], idf, 0), (idb_s[:], idb, 0), (onescol_s[:], onescol, 0),
                (ones1b_s[:], ones1b, 0), (ones1f_s[:], ones1f, 0),
                (onehots_s[:], onehots_d, 0),
                (Ctab_s[:], Ctab_d, 0), (bqkq_s[:], bqkq, 0), (bqkk_s[:], bqkk, 0),
                (bg_s[:], bg, 0), (bo_s[:], bo, 0),
                (fqb_s[:], fqb, 0), (fkb_s[:], fkb, 0),
                (fqbs_s[:], fqbs, 0), (fkbs_s[:], fkbs, 0),
                (fob_s[:], fob, 0), (n1w_s[:], n1w, 0), (n2w_s[:], n2w, 0),
                (fc_s[:], fconst, 0)):
            cload(dst, src, cast)
        for e in range(E):
            P.dma('io', lambda eng, d=w3bf[2048 * e:2048 * e + 2048, :], s=w3T[e]:
                  eng.dma_start(out=d, in_=s), 'w3c',
                  writes=[RD('w3bf', 0, 128, 2048 * e, 2048 * e + 2048)], tag='w3c')

        for tau in range(32):
            pb = tau % 2
            xb = FAR[:, 1024 * pb:1024 * pb + 1024]
            xbr = RD('far', 0, 128, 1024 * pb, 1024 * pb + 1024)
            xnf = FAR[:, 2048 + 1024 * pb:3072 + 1024 * pb]
            xnfr = RD('far', 0, 128, 2048 + 1024 * pb, 3072 + 1024 * pb)
            sq = ZAR[:, 1024 * pb:1024 * pb + 1024]
            sqr = RD('zar', 0, 128, 1024 * pb, 1024 * pb + 1024)
            xnb = ZAR[:, 2048 + 1024 * pb:3072 + 1024 * pb]
            xnbr = RD('zar', 0, 128, 2048 + 1024 * pb, 3072 + 1024 * pb)
            xst = QAR[:, 1024 * pb:1024 * pb + 1024]
            xstr = RD('qar', 0, 128, 1024 * pb, 1024 * pb + 1024)
            c0 = 8 * pb
            def col(i, c0=c0):
                return (cols[:, c0 + i:c0 + i + 1],
                        RD('cols', 0, 128, c0 + i, c0 + i + 1))
            P.dma('sp', lambda eng, d=xb, s=xp[128 * tau:128 * tau + 128, :]:
                  eng.dma_start(out=d, in_=s), f'xin{pb}', writes=[xbr], tag='xA')
            s1, s1r = col(0)
            P.op('dve', lambda eng, o=s1, i=xb: eng.reduce_sum(o, i, AX.X),
                 reads=[xbr], writes=[s1r])
            s2, s2r = col(1)
            P.op('act', lambda eng, o=sq, i=xb, a=s2:
                 eng.activation(o, i, AF.Square, bias=0.0, scale=1.0, accum_out=a),
                 reads=[xbr], writes=[sqr, s2r])
            tt, ttr = col(2)
            P.op('dve', lambda eng, o=tt, i=s1: eng.tensor_mul(o, i, i),
                 reads=[s1r], writes=[ttr])
            u, ur = col(3)
            P.op('dve', lambda eng, o=u, i=tt, j=s2:
                 eng.scalar_tensor_tensor(o, i, -1.0 / 1024, j, ALU.mult, ALU.add),
                 reads=[ttr, s2r], writes=[ur])
            sdv, sdr = col(4)
            P.op('act', lambda eng, o=sdv, i=u:
                 eng.activation(o, i, AF.Sqrt, bias=fc_s[:, 0:1], scale=1.0 / 1024),
                 reads=[ur], writes=[sdr])
            rstd, rstdr = col(5)
            P.op('dve', lambda eng, o=rstd, i=sdv: eng.reciprocal(o, i),
                 reads=[sdr], writes=[rstdr])
            negm, negmr = col(6)
            P.op('dve', lambda eng, o=negm, i=s1:
                 eng.tensor_scalar_mul(o, i, -1.0 / 1024),
                 reads=[s1r], writes=[negmr])
            P.op('dve', lambda eng, o=xnf, i=xb, a=negm, b=rstd:
                 eng.tensor_scalar(o, i, a, b, ALU.add, ALU.mult),
                 reads=[xbr, negmr, rstdr], writes=[xnfr])
            aa, aar = col(7)
            P.op('dve', lambda eng, o=aa, i=u: eng.tensor_scalar_add(o, i, 1024 * LN_EPS),
                 reads=[ur], writes=[aar])
            P.op('dve', lambda eng, o=tt, i=aa: eng.reciprocal(o, i),
                 reads=[aar], writes=[ttr])
            P.op('dve', lambda eng, o=aa, i=u, j=tt: eng.tensor_mul(o, i, j),
                 reads=[ur, ttr], writes=[aar])
            P.op('act', lambda eng, o=tt, i=aa:
                 eng.activation(o, i, AF.Sqrt, bias=fc_s[:, 1:2], scale=1.0),
                 reads=[aar], writes=[ttr])
            P.op('dve', lambda eng, o=aa, i=tt: eng.reciprocal(o, i),
                 reads=[ttr], writes=[aar])
            def _tb(i):
                if i < 4:
                    return (PSA[:, 1536 + 128 * i:1536 + 128 * i + 128],
                            RD('psa', 0, 128, 1536 + 128 * i, 1536 + 128 * i + 128))
                return (PSP[:, 128 * (i - 4):128 * (i - 4) + 128],
                        RD('psp', 0, 128, 128 * (i - 4), 128 * (i - 4) + 128))
            for i in range(8):
                tb, tbr = _tb(i)
                P.op('pe', lambda eng, o=tb, a=xnf[:, 128 * i:128 * i + 128],
                     idn=idf_s[:]: eng.transpose(o, a, idn),
                     reads=[xnfr] + CST, writes=[tbr])
            for i in range(8):
                tb, tbr = _tb(i)
                P.op('dve', lambda eng, o=AB[i][:, 128 * tau:128 * tau + 128],
                     s=tb: eng.tensor_copy(o, s),
                     reads=[tbr],
                     writes=[RD(f'ab{i}', 0, 128, 128 * tau, 128 * tau + 128)])
            P.op('pe', lambda eng, o=ps7[0:1, 0:128], a=aa, idn=idf_s[:]:
                 eng.transpose(o, a, idn), reads=[aar] + CST, writes=[PS7R])
            P.op('act', lambda eng, o=rows[3][0:1, 0:128], s=ps7[0:1, 0:128]:
                 eng.activation(o, s, AF.Identity, bias=0.0, scale=1.0),
                 reads=[PS7R], writes=[RD('row3', 0, 1, 0, 128)])
            P.op('pe', lambda eng, o=ps[5][:, 0:128], l=ones1f_s[:],
                 r=rows[3][0:1, 0:128]:
                 eng.matmul(o, l, r, start=True, stop=True),
                 reads=[RD('row3', 0, 1, 0, 128)] + CST, writes=[PSR(5)])
            for i in range(8):
                P.op('dve', lambda eng, o=xst[:, 128 * i:128 * i + 128],
                     a=AB[i][:, 128 * tau:128 * tau + 128],
                     w=n1w_s[:, i:i + 1], b=ps[5][:, 0:128]:
                     eng.scalar_tensor_tensor(o, a, w, b, ALU.mult, ALU.mult),
                     reads=[RD(f'ab{i}', 0, 128, 128 * tau, 128 * tau + 128),
                            PSR(5), RD('cst', 0, 1, 0, 1)],
                     writes=[xstr])
            P.dma('act', lambda eng, d=xnr1d[:, 128 * tau:128 * tau + 128]
                  .rearrange("(ic p) t -> p ic t", p=128),
                  s=xst.rearrange("p (ic t) -> p ic t", t=128):
                  eng.dma_start(out=d, in_=s), 'xnr1w', reads=[xstr],
                  writes=[RD('xnr1d', 0, 128, 128 * tau, 128 * tau + 128)])
            if (tau // 4) % 2 == 0:
                om = (tau // 8) * 4 + tau % 4
                stg = SCR[:, 1024 * pb:1024 * pb + 1024]
                stgr = RD('scr', 0, 128, 1024 * pb, 1024 * pb + 1024)
                for i in range(8):
                    bk, bn = (ps[3], 3) if i < 4 else (ps[4], 4)
                    fo = 128 * (i % 4)
                    P.op('pe', lambda eng, o=bk[:, fo:fo + 128],
                         a=xnf[:, 128 * i:128 * i + 128], idn=idf_s[:]:
                         eng.transpose(o, a, idn),
                         reads=[xnfr] + CST, writes=[PSR(bn)])
                for i in range(8):
                    bk, bn = (ps[3], 3) if i < 4 else (ps[4], 4)
                    fo = 128 * (i % 4)
                    P.op('act', lambda eng, o=stg[:, 128 * i:128 * i + 128],
                         s=bk[:, fo:fo + 128]:
                         eng.activation(o, s, AF.Identity, bias=0.0, scale=1.0),
                         reads=[PSR(bn)], writes=[stgr])
                P.dma('act', lambda eng, d=resd[:, 128 * om:128 * om + 128]
                      .rearrange("(ic p) t -> p ic t", p=128),
                      s=stg.rearrange("p (ic t) -> p ic t", t=128):
                      eng.dma_start(out=d, in_=s), 'resw', reads=[stgr],
                      writes=[RD('resd', 0, 128, 128 * om, 128 * om + 128)])

        if _PHASES >= 2:
            WARr = lambda a, b: RD('war', 0, 128, a, b)
            def wload_b(dst, srct, a, b):
                P.dma('io', lambda eng, d=dst, s=srct:
                      eng.dma_start(out=d, in_=s), 'wb',
                      writes=[WARr(a, b)], tag='wB')
            def loadw_early(pr):
                wload_b(WAR[:, 0:2048].rearrange("p (ic c) -> p ic c", c=256),
                        wqkp[pr], 0, 2048)
                wload_b(WAR[:, 2048:4096].rearrange("p (ic c) -> p ic c", c=256),
                        wvp[pr], 2048, 4096)
                P.dma('io', lambda eng, d=bvp[:],
                      s=bvrow[None, 256 * pr:256 * pr + 256]:
                      eng.dma_start(out=d, in_=s), 'bvp',
                      writes=[RD('bvp', 0, 1, 0, 256)])
            def loadw_late(pr):
                wload_b(WAR[:, 4096:6144].rearrange("p (ic c) -> p ic c", c=256),
                        wgp[pr], 4096, 6144)
                wload_b(WAR[:, 6144:8192].rearrange("p (ic c) -> p ic c", c=1024),
                        wop[pr], 6144, 8192)
            loadw_early(0)
            for pr in range(_B_PAIRS):
                h0 = 2 * pr
                loadw_late(pr)
                for hh in range(2):
                    P.dma('sp', lambda eng, d=krow[:], s=kscd[h0 + hh:h0 + hh + 1, :]:
                          eng.dma_start(out=d, in_=s), 'krow',
                          writes=[RD('krow', 0, 1, 0, 512)])
                    P.op('pe', lambda eng, o=ps[5][64 * hh:64 * hh + 64, :],
                         l=ones1b_s[0:1, 0:64], r=krow[:]:
                         eng.matmul(o, l, r, start=True, stop=True),
                         reads=[RD('krow', 0, 1, 0, 512)] + CST, writes=[PSR(5)])
                P.op('act', lambda eng, o=ksct[:], s=ps[5][:]:
                     eng.activation(o, s, AF.Identity, bias=0.0, scale=1.0),
                     reads=[PSR(5)], writes=[RD('ksct', 0, 128, 0, 512)])
                for stile in range(2):
                    f0 = 512 * stile
                    kb, kbn = (ps[1], 1) if stile % 2 == 0 else (ps[2], 2)
                    for ic in range(8):
                        P.op('pe', lambda eng, o=kb[:],
                             l=WAR[:, 256 * ic + 128:256 * ic + 256],
                             r=AB[ic][:, f0:f0 + 512], s=(ic == 0), t=(ic == 7):
                             eng.matmul(o, l, r, start=s, stop=t),
                             reads=[WARr(0, 2048), RD(f'ab{ic}', 0, 128, f0, f0 + 512)],
                             writes=[PSR(kbn)])
                    P.op('dve', lambda eng, o=KAR[:, f0:f0 + 512], i=kb[:],
                         b=bqkk_s[:, pr:pr + 1], k=ksct[:]:
                         eng.scalar_tensor_tensor(o, i, b, k, ALU.add, ALU.mult),
                         reads=[PSR(kbn), RD('ksct', 0, 128, 0, 512),
                                RD('cst', 0, 1, 0, 1)],
                         writes=[RD('kar', 0, 128, f0, f0 + 512)])
                for sc in range(8):
                    vb, vbr = (ps[0], PSR(0)) if sc % 2 == 0 else (ps7, PS7R)
                    for ic in range(8):
                        P.op('pe', lambda eng, o=vb[:, 0:256],
                             l=AB[ic][:, 128 * sc:128 * sc + 128],
                             r=WAR[:, 2048 + 256 * ic:2048 + 256 * ic + 256],
                             s=(ic == 0): eng.matmul(o, l, r, start=s, stop=False),
                             reads=[WARr(2048, 4096),
                                    RD(f'ab{ic}', 0, 128, 128 * sc, 128 * sc + 128)],
                             writes=[vbr])
                    P.op('pe', lambda eng, o=vb[:, 0:256], l=ones1b_s[:],
                         r=bvp[:]:
                         eng.matmul(o, l, r, start=False, stop=True),
                         reads=[RD('bvp', 0, 1, 0, 256)] + CST, writes=[vbr])
                    P.op('dve', lambda eng, o=VAR[:, 256 * sc:256 * sc + 256],
                         i=vb[:, 0:256]: eng.tensor_copy(o, i),
                         reads=[vbr],
                         writes=[RD('var', 0, 128, 256 * sc, 256 * sc + 256)])
                for j in range(1):
                    q0 = 1024 * j
                    for hh in range(2):
                        ri = (h0 + hh) * 4 + j
                        P.dma('sp', lambda eng, d=qrow[:], s=qscd[ri:ri + 1, :]:
                              eng.dma_start(out=d, in_=s), 'qrow',
                              writes=[RD('krow', 0, 1, 0, 512)])
                        P.op('pe', lambda eng, o=ps[5][64 * hh:64 * hh + 64, :],
                             l=ones1b_s[0:1, 0:64], r=qrow[:]:
                             eng.matmul(o, l, r, start=True, stop=True),
                             reads=[RD('krow', 0, 1, 0, 512)] + CST, writes=[PSR(5)])
                    P.op('act', lambda eng, o=qsct[:], s=ps[5][:]:
                         eng.activation(o, s, AF.Identity, bias=0.0, scale=1.0),
                         reads=[PSR(5)], writes=[RD('qsct', 0, 128, 0, 512)])
                    for ic in range(8):
                        P.op('pe', lambda eng, o=ps[1][:],
                             l=WAR[:, 256 * ic:256 * ic + 128],
                             r=AB[ic][:, q0:q0 + 512], s=(ic == 0), t=(ic == 7):
                             eng.matmul(o, l, r, start=s, stop=t),
                             reads=[WARr(0, 2048), RD(f'ab{ic}', 0, 128, q0, q0 + 512)],
                             writes=[PSR(1)])
                    P.op('dve', lambda eng, o=QAR[:, 512 * j:512 * j + 512],
                         i=ps[1][:], b=bqkq_s[:, pr:pr + 1], k=qsct[:]:
                         eng.scalar_tensor_tensor(o, i, b, k, ALU.add, ALU.mult),
                         reads=[PSR(1), RD('qsct', 0, 128, 0, 512),
                                RD('cst', 0, 1, 0, 1)],
                         writes=[RD('qar', 0, 128, 512 * j, 512 * j + 512)])
                PSV1 = PSV[:, 512:1024]
                PSV1R = RD('psv', 0, 128, 512, 1024)
                bfill = []
                bdone = [0]
                def bpump(n=1):
                    for _ in range(min(n, len(bfill))):
                        bfill.pop(0)()
                        bdone[0] += 1
                def bdrain(target):
                    while bdone[0] < target and bfill:
                        bfill.pop(0)()
                        bdone[0] += 1
                def q_kstile(stile):
                    f0 = 512 * stile
                    for ic0 in (0, 4):
                        def mmk(ic0=ic0, f0=f0):
                            for ic in range(ic0, ic0 + 4):
                                P.op('pe', lambda eng, o=PSV1,
                                     l=WAR[:, 256 * ic + 128:256 * ic + 256],
                                     r=AB[ic][:, f0:f0 + 512],
                                     s=(ic == 0), t=(ic == 7):
                                     eng.matmul(o, l, r, start=s, stop=t),
                                     reads=[WARr(0, 2048),
                                            RD(f'ab{ic}', 0, 128, f0, f0 + 512)],
                                     writes=[PSV1R])
                        bfill.append(mmk)
                    def evk(f0=f0):
                        P.op('dve', lambda eng, o=KAR[:, f0:f0 + 512], i=PSV1,
                             b=bqkk_s[:, pr:pr + 1], k=ksct[:]:
                             eng.scalar_tensor_tensor(o, i, b, k, ALU.add,
                                                      ALU.mult),
                             reads=[PSV1R, RD('ksct', 0, 128, 0, 512),
                                    RD('cst', 0, 1, 0, 1)],
                             writes=[RD('kar', 0, 128, f0, f0 + 512)])
                    bfill.append(evk)
                def q_vchunk(sc):
                    def mmva(sc=sc):
                        for ic in range(4):
                            P.op('pe', lambda eng, o=PSV1[:, 0:256],
                                 l=AB[ic][:, 128 * sc:128 * sc + 128],
                                 r=WAR[:, 2048 + 256 * ic:2048 + 256 * ic + 256],
                                 s=(ic == 0):
                                 eng.matmul(o, l, r, start=s, stop=False),
                                 reads=[WARr(2048, 4096),
                                        RD(f'ab{ic}', 0, 128, 128 * sc,
                                           128 * sc + 128)],
                                 writes=[PSV1R])
                    def mmvb(sc=sc):
                        for ic in range(4, 8):
                            P.op('pe', lambda eng, o=PSV1[:, 0:256],
                                 l=AB[ic][:, 128 * sc:128 * sc + 128],
                                 r=WAR[:, 2048 + 256 * ic:2048 + 256 * ic + 256]:
                                 eng.matmul(o, l, r, start=False, stop=False),
                                 reads=[WARr(2048, 4096),
                                        RD(f'ab{ic}', 0, 128, 128 * sc,
                                           128 * sc + 128)],
                                 writes=[PSV1R])
                        P.op('pe', lambda eng, o=PSV1[:, 0:256], l=ones1b_s[:],
                             r=bvp[:]:
                             eng.matmul(o, l, r, start=False, stop=True),
                             reads=[RD('bvp', 0, 1, 0, 256)] + CST,
                             writes=[PSV1R])
                    def evv(sc=sc):
                        P.op('dve', lambda eng, o=VAR[:, 256 * sc:256 * sc + 256],
                             i=PSV1[:, 0:256]: eng.tensor_copy(o, i),
                             reads=[PSV1R],
                             writes=[RD('var', 0, 128, 256 * sc, 256 * sc + 256)])
                    bfill.extend((mmva, mmvb, evv))
                def q_qslot(j):
                    q0 = 1024 * j
                    def qsb(j=j):
                        for hh in range(2):
                            ri = (h0 + hh) * 4 + j
                            P.dma('sp', lambda eng, d=qrow[:],
                                  s=qscd[ri:ri + 1, :]:
                                  eng.dma_start(out=d, in_=s), 'qrow',
                                  writes=[RD('krow', 0, 1, 0, 512)])
                            P.op('pe', lambda eng,
                                 o=PSV1[64 * hh:64 * hh + 64, :],
                                 l=ones1b_s[0:1, 0:64], r=qrow[:]:
                                 eng.matmul(o, l, r, start=True, stop=True),
                                 reads=[RD('krow', 0, 1, 0, 512)] + CST,
                                 writes=[PSV1R])
                    def qse():
                        P.op('act', lambda eng, o=qsct[:], s=PSV1:
                             eng.activation(o, s, AF.Identity, bias=0.0,
                                            scale=1.0),
                             reads=[PSV1R], writes=[RD('qsct', 0, 128, 0, 512)])
                    for ic0 in (0, 4):
                        def mmq(ic0=ic0, q0=q0):
                            for ic in range(ic0, ic0 + 4):
                                P.op('pe', lambda eng, o=PSV1,
                                     l=WAR[:, 256 * ic:256 * ic + 128],
                                     r=AB[ic][:, q0:q0 + 512],
                                     s=(ic == 0), t=(ic == 7):
                                     eng.matmul(o, l, r, start=s, stop=t),
                                     reads=[WARr(0, 2048),
                                            RD(f'ab{ic}', 0, 128, q0, q0 + 512)],
                                     writes=[PSV1R])
                        if ic0 == 0:
                            bfill.extend((qsb, qse, mmq))
                        else:
                            bfill.append(mmq)
                    def evq(j=j):
                        P.op('dve', lambda eng, o=QAR[:, 512 * j:512 * j + 512],
                             i=PSV1, b=bqkq_s[:, pr:pr + 1], k=qsct[:]:
                             eng.scalar_tensor_tensor(o, i, b, k, ALU.add,
                                                      ALU.mult),
                             reads=[PSV1R, RD('qsct', 0, 128, 0, 512),
                                    RD('cst', 0, 1, 0, 1)],
                             writes=[RD('qar', 0, 128, 512 * j,
                                        512 * j + 512)])
                    bfill.append(evq)
                q_kstile(2); q_kstile(3); q_qslot(1)
                for sc in range(8, 16):
                    q_vchunk(sc)
                q_kstile(4); q_kstile(5); q_qslot(2)
                for sc in range(16, 24):
                    q_vchunk(sc)
                q_kstile(6); q_kstile(7); q_qslot(3)
                for sc in range(24, 32):
                    q_vchunk(sc)
                BDL = {2: 12, 3: 36, 4: 48, 5: 72, 6: 84, 7: 108}
                pend = []
                def pop_pend(nmax):
                    for _ in range(min(nmax, len(pend))):
                        pend.pop(0)()
                prevu = [None]
                tail_hold = [None]
                def emit_av(c):
                    pj, phh, plim, pp = prevu[0]
                    P.op('pe', lambda eng, o=ps[2][:],
                         l=VAR[:, 256 * c + 128 * phh:256 * c + 128 * phh + 128],
                         r=ZAR[:, 512 * c:512 * c + 512], s=(c == 0),
                         t=(c == plim - 1):
                         eng.matmul(o, l, r, start=s, stop=t),
                         reads=[RD('var', 0, 128, 256 * c + 128 * phh,
                                   256 * c + 128 * phh + 128),
                                RD('zar', 0, 128, 512 * c, 512 * c + 512)],
                         writes=[PSR(2)])
                def finish_prev():
                    while pend:
                        pend.pop(0)()
                    P.op('dve', lambda eng, o=SCR[:, 512:1024], i=ps[2][:]:
                         eng.tensor_copy(o, i),
                         reads=[PSR(2)], writes=[RD('scr', 0, 128, 512, 1024)])
                    pend.extend(tail_hold[0]())
                unit_idx = [0]
                for j in range(4 if _B_STAGE >= 2 else 0):
                    limit = 8 * (j + 1)
                    P.dma('sp', lambda eng,
                          d=MAT[:].rearrange("p (lc t) -> p lc t", t=512),
                          s=mask8_d[j].rearrange("lc p t -> p lc t"):
                          eng.dma_start(out=d, in_=s), 'msk',
                          writes=[RD('mat', 0, 128, 0, 4096)], tag='mskB')
                    for hh in range(2):
                        h = h0 + hh
                        hb = 64 * hh
                        u_i = unit_idx[0]
                        bdrain(BDL.get(u_i, 0))
                        if pr + 1 < _B_PAIRS and u_i == 6:
                            wload_b(WAR[:, 0:2048]
                                    .rearrange("p (ic c) -> p ic c", c=256),
                                    wqkp[pr + 1], 0, 2048)
                        if pr + 1 < _B_PAIRS and u_i == 7:
                            wload_b(WAR[:, 2048:4096]
                                    .rearrange("p (ic c) -> p ic c", c=256),
                                    wvp[pr + 1], 2048, 4096)
                            P.dma('io', lambda eng, d=bvp[:],
                                  s=bvrow[None, 256 * (pr + 1):
                                          256 * (pr + 1) + 256]:
                                  eng.dma_start(out=d, in_=s), 'bvp',
                                  writes=[RD('bvp', 0, 1, 0, 256)])
                        par = unit_idx[0] % 2
                        unit_idx[0] += 1
                        dclv = SCR[0:1, 1024 + 512 * par:1536 + 512 * par]
                        dclr = RD('scr', 0, 1, 1024 + 512 * par, 1536 + 512 * par)
                        qr = RD('qar', hb, hb + 64, 512 * j, 512 * j + 512)
                        def ones_mm(c, limit=limit):
                            at = bt[c % 4]
                            atr = BTR(c % 4)
                            P.op('pe', lambda eng, o=ps[1][0:1, :], l=onescol_s[:],
                                 r=at[:], s=(c == 0), t=(c == limit - 1):
                                 eng.matmul(o, l, r, start=s, stop=t),
                                 reads=[atr] + CST, writes=[PSR(1)])
                        plim = prevu[0][2] if prevu[0] is not None else 0
                        for c in range(max(limit, plim)):
                            bpump(1)
                            if prevu[0] is not None and c < plim:
                                emit_av(c)
                            if c >= limit:
                                pop_pend(2)
                                continue
                            sb_, sbn = (ps[0], 0) if c % 2 == 0 else (ps7, 7)
                            sbr = PSR(sbn)
                            P.op('pe', lambda eng, o=sb_[:],
                                 l=KAR[hb:hb + 64, 128 * c:128 * c + 128],
                                 r=QAR[hb:hb + 64, 512 * j:512 * j + 512]:
                                 eng.matmul(o, l, r, start=True, stop=True),
                                 reads=[RD('kar', hb, hb + 64, 128 * c, 128 * c + 128),
                                        qr],
                                 writes=[sbr])
                            ci = h * 128 + j * 32 + c
                            Sw = RD('zar', 0, 128, 512 * c, 512 * c + 512)
                            if c < 8 * j:
                                P.op('dve', lambda eng,
                                     o=ZAR[:, 512 * c:512 * c + 512], i=sb_[:],
                                     sc=Ctab_s[:, ci:ci + 1]:
                                     eng.tensor_scalar(o, i, sc, None, ALU.mult),
                                     reads=[sbr, RD('Ctab_s', 0, 128, ci, ci + 1)],
                                     writes=[Sw])
                            else:
                                lc = c - 8 * j
                                P.op('dve', lambda eng,
                                     o=ZAR[:, 512 * c:512 * c + 512], i=sb_[:],
                                     sc=Ctab_s[:, ci:ci + 1],
                                     m=MAT[:, 512 * lc:512 * lc + 512]:
                                     eng.scalar_tensor_tensor(o, i, sc, m,
                                                              ALU.mult, ALU.mult),
                                     reads=[sbr, RD('Ctab_s', 0, 128, ci, ci + 1),
                                            RD('mat', 0, 128, 512 * lc, 512 * lc + 512)],
                                     writes=[Sw])
                            if c % 2 == 1:
                                a0 = 512 * ((c - 1) % 4)
                                P.op('act', lambda eng, o=BT2[:, a0:a0 + 1024],
                                     i=ZAR[:, 512 * c - 512:512 * c + 512]:
                                     eng.activation(o, i, AF.Abs, bias=0.0,
                                                    scale=1.0),
                                     reads=[RD('zar', 0, 128, 512 * c - 512,
                                               512 * c + 512)],
                                     writes=[RD('bt2c', 0, 128, a0, a0 + 1024)])
                            if c >= 3:
                                ones_mm(c - 3)
                            pop_pend(2)
                        if prevu[0] is not None:
                            finish_prev()
                        ones_mm(limit - 3)
                        ones_mm(limit - 2)
                        ones_mm(limit - 1)
                        P.op('dve', lambda eng, o=dclv, i=ps[1][0:1, :]:
                             eng.tensor_scalar(o, i, 1.0, 50000.0, ALU.max, ALU.min),
                             reads=[PSR(1)], writes=[dclr])
                        prevu[0] = (j, hh, limit, par)
                        def tail(j=j, hh=hh, h=h, dclv=dclv, dclr=dclr):
                            T = []
                            T.append(lambda: P.op('act', lambda eng, o=bt[4][:],
                                 i=SCR[:, 512:1024]:
                                 eng.activation(o, i, AF.Square, bias=0.0, scale=1.0),
                                 reads=[RD('scr', 0, 128, 512, 1024)],
                                 writes=[RD('bt4', 0, 128, 0, 512)]))
                            T.append(lambda: P.op('pe', lambda eng, o=ps[3][0:1, :],
                                 l=onescol_s[:], r=bt[4][:]:
                                 eng.matmul(o, l, r, start=True, stop=True),
                                 reads=[RD('bt4', 0, 128, 0, 512)] + CST,
                                 writes=[PSR(3)]))
                            for ic in range(8):
                                T.append(lambda ic=ic: P.op('pe',
                                     lambda eng, o=ps[5][:],
                                     l=WAR[:, 4096 + 256 * ic + 128 * hh:
                                            4096 + 256 * ic + 128 * hh + 128],
                                     r=AB[ic][:, 1024 * j:1024 * j + 512],
                                     s=(ic == 0), t=(ic == 7):
                                     eng.matmul(o, l, r, start=s, stop=t),
                                     reads=[WARr(4096, 6144),
                                            RD(f'ab{ic}', 0, 128, 1024 * j,
                                               1024 * j + 512)],
                                     writes=[PSR(5)]))
                            T.append(lambda: P.op('dve', lambda eng, o=rows[1][:],
                                 i=dclv, j2=dclv:
                                 eng.scalar_tensor_tensor(o, i, EPS_RET, j2,
                                                          ALU.mult, ALU.mult),
                                 reads=[dclr],
                                 writes=[RD('row1', 0, 1, 0, 512)]))
                            T.append(lambda: P.op('dve', lambda eng, o=rows[2][:],
                                 i=ps[3][0:1, :], j2=rows[1][:]:
                                 eng.scalar_tensor_tensor(o, i, 1.0 / 128, j2,
                                                          ALU.mult, ALU.add),
                                 reads=[PSR(3), RD('row1', 0, 1, 0, 512)],
                                 writes=[RD('row2', 0, 1, 0, 512)]))
                            T.append(lambda: P.op('act', lambda eng, o=rows[1][:],
                                 i=rows[2][:]:
                                 eng.activation(o, i, AF.Ln, bias=0.0, scale=1.0),
                                 reads=[RD('row2', 0, 1, 0, 512)],
                                 writes=[RD('row1', 0, 1, 0, 512)]))
                            T.append(lambda: P.op('act', lambda eng, o=rows[2][:],
                                 i=rows[1][:]:
                                 eng.activation(o, i, AF.Exp, bias=0.0, scale=-0.5),
                                 reads=[RD('row1', 0, 1, 0, 512)],
                                 writes=[RD('row2', 0, 1, 0, 512)]))
                            T.append(lambda: P.op('pe', lambda eng, o=ps[4][:],
                                 l=ones1f_s[:], r=rows[2][:]:
                                 eng.matmul(o, l, r, start=True, stop=True),
                                 reads=[RD('row2', 0, 1, 0, 512)] + CST,
                                 writes=[PSR(4)]))
                            T.append(lambda: P.op('act', lambda eng, o=SCR[:, 0:512],
                                 s=ps[4][:]:
                                 eng.activation(o, s, AF.Identity, bias=0.0, scale=1.0),
                                 reads=[PSR(4)], writes=[RD('scr', 0, 128, 0, 512)]))
                            T.append(lambda: P.op('dve', lambda eng, o=bt[4][:],
                                 i=SCR[:, 512:1024], j2=SCR[:, 0:512]:
                                 eng.tensor_mul(o, i, j2),
                                 reads=[RD('scr', 0, 128, 512, 1024),
                                        RD('scr', 0, 128, 0, 512)],
                                 writes=[RD('bt4', 0, 128, 0, 512)]))
                            T.append(lambda: P.op('dve', lambda eng, o=bt5[:],
                                 i=ps[5][:], b=bg_s[:, h:h + 1], y=bt[4][:]:
                                 eng.scalar_tensor_tensor(o, i, b, y, ALU.add, ALU.mult),
                                 reads=[PSR(5), RD('bt4', 0, 128, 0, 512),
                                        RD('cst', 0, 1, 0, 1)],
                                 writes=[RD('bt5', 0, 128, 0, 512)]))
                            for oc in range(8):
                                bk, bn = (ps[4], 4) if oc % 2 == 0 else (ps[3], 3)
                                T.append(lambda oc=oc, bk=bk, bn=bn: P.op('pe',
                                     lambda eng, o=bk[:],
                                     l=WAR[:, 6144 + 1024 * hh + 128 * oc:
                                            6144 + 1024 * hh + 128 * oc + 128],
                                     r=bt5[:]: eng.matmul(o, l, r, start=True, stop=True),
                                     reads=[WARr(6144, 8192),
                                            RD('bt5', 0, 128, 0, 512)],
                                     writes=[PSR(bn)]))
                                sw = RD('far', 0, 128, 512 * oc, 512 * oc + 512)
                                if hh == 0:
                                    T.append(lambda oc=oc, bk=bk, bn=bn, sw=sw:
                                         P.op('dve', lambda eng,
                                              o=FAR[:, 512 * oc:512 * oc + 512],
                                              i=bk[:]: eng.tensor_copy(o, i),
                                              reads=[PSR(bn)], writes=[sw]))
                                else:
                                    T.append(lambda oc=oc, bk=bk, bn=bn, sw=sw:
                                         P.op('dve', lambda eng,
                                              o=FAR[:, 512 * oc:512 * oc + 512],
                                              i=bk[:],
                                              j2=FAR[:, 512 * oc:512 * oc + 512]:
                                              eng.tensor_add(o, i, j2),
                                              reads=[PSR(bn), sw], writes=[sw]))
                            if hh == 1:
                                aop = ((ALU.bypass if pr == 0 else ALU.add)
                                       if _B_SACC else ALU.bypass)
                                T.append(lambda aop=aop: P.dma('io', lambda eng,
                                     d=sd[:, 512 * j:512 * j + 512]
                                     .rearrange("(ic p) t -> p ic t", p=128),
                                     s=FAR[:].rearrange("p (ic t) -> p ic t", t=512),
                                     o=aop: eng.dma_start(out=d, in_=s, accum_op=o),
                                     'sacc',
                                     reads=[RD('far', 0, 128, 0, 4096)],
                                     writes=[RD('sd', 0, 128, 512 * j, 512 * j + 512)],
                                     tag='sacc'))
                            return T
                        tail_hold[0] = tail
                if prevu[0] is not None:
                    plim = prevu[0][2]
                    for c in range(plim):
                        emit_av(c)
                    finish_prev()
                    while pend:
                        pend.pop(0)()
                    prevu[0] = None
                bdrain(1 << 30)

        if _PHASES >= 3:
            WARr = lambda a, b: RD('war', 0, 128, a, b)
            ZR = lambda a, b: RD('zar', 0, 128, a, b)
            for i in range(8):
                P.dma('sp' if i % 2 == 0 else 'io',
                      lambda eng, d=AB[i][:, 0:4096],
                      s=xnr1d[128 * i:128 * i + 128, :]:
                      eng.dma_start(out=d, in_=s), 'xn1l',
                      reads=[RD('xnr1d', 0, 128, 0, 4096)],
                      writes=[RD(f'ab{i}', 0, 128, 0, 4096)], tag='xn1l')
            RKC, RKS = 2048, 6144
            for dst0, srct, ln in ((RKC, rkc, 4096), (RKS, rks, 4096)):
                P.dma('io', lambda eng, d=ZAR[:, dst0:dst0 + ln], s=srct:
                      eng.dma_start(out=d, in_=s), 'ropl',
                      writes=[ZR(dst0, dst0 + ln)], tag='ropl')
            P.op('dve', lambda eng,
                 o=VAR[:, 0:8320].rearrange("p (c x) -> p c x", x=65)[:, :, 64:65]:
                 eng.memset(o, 1.0),
                 reads=(), writes=[RD('var', 0, 128, 0, 8320)])
            KBUF = [(KAR, 'kar'), (KAR2, 'Ctab_s')]
            QBUF = [(QAR, 'qar', 0), (ZAR, 'zar', 14336)]
            unit_ctr = [0]
            pend_fin = [None]
            filler = []
            def pump(n=1):
                for _ in range(min(n, len(filler))):
                    filler.pop(0)()
            def drain():
                while filler:
                    filler.pop(0)()

            def load_qkw(g):
                P.dma('io', lambda eng,
                      d=WAR[:, 0:8192].rearrange("p (s ic c) -> p s ic c",
                                                 s=4, c=256),
                      s=fqp[g]: eng.dma_start(out=d, in_=s), 'wc',
                      writes=[WARr(0, 8192)], tag='wC')
                P.dma('io', lambda eng,
                      d=ZAR[:, 12288:14336].rearrange("p (ic c) -> p ic c", c=256),
                      s=fksp[g]: eng.dma_start(out=d, in_=s), 'wc',
                      writes=[ZR(12288, 14336)], tag='wC')
            def load_ow(g):
                P.dma('io', lambda eng,
                      d=ZAR[:, 10240:12288].rearrange("p (ic c) -> p ic c", c=1024),
                      s=fop[g]: eng.dma_start(out=d, in_=s), 'wc',
                      writes=[ZR(10240, 12288)], tag='wC')

            def v_loop(g):
                P.dma('io', lambda eng, d=bvp[:],
                      s=fvb[None, 256 * g:256 * g + 256]:
                      eng.dma_start(out=d, in_=s), 'bvp',
                      writes=[RD('bvp', 0, 1, 0, 256)])
                for sc in range(32):
                    vb, vbr = (ps[0], PSR(0)) if sc % 2 == 0 else (ps7, PS7R)
                    for ic in range(8):
                        P.op('pe', lambda eng, o=vb[:, 0:256],
                             l=AB[ic][:, 128 * sc:128 * sc + 128],
                             r=WAR[:, 4096 + 256 * ic:4096 + 256 * ic + 256],
                             s=(ic == 0): eng.matmul(o, l, r, start=s, stop=False),
                             reads=[WARr(4096, 6144),
                                    RD(f'ab{ic}', 0, 128, 128 * sc, 128 * sc + 128)],
                             writes=[vbr])
                    P.op('pe', lambda eng, o=vb[:, 0:256], l=ones1b_s[:],
                         r=bvp[:]:
                         eng.matmul(o, l, r, start=False, stop=True),
                         reads=[RD('bvp', 0, 1, 0, 256)] + CST, writes=[vbr])
                    for hg in range(4):
                        P.op('dve', lambda eng,
                             o=VAR[:, 260 * sc + 65 * hg:260 * sc + 65 * hg + 64],
                             i=vb[:, 64 * hg:64 * hg + 64]:
                             eng.tensor_copy(o, i),
                             reads=[vbr],
                             writes=[RD('var', 0, 128, 260 * sc + 65 * hg,
                                        260 * sc + 65 * hg + 64)])

            def _targets(g, duo, KB, QB):
                kt, kn = KB
                qt, qn, qoff = QB
                tg = []
                for stile in range(8):
                    tg.append((512 * stile, 2048, (ZAR, 'zar', 12288),
                               fkb_s, fkbs_s, RKC + 512 * stile, RKS + 512 * stile,
                               kt, kn, 512 * stile))
                for j in range(4):
                    tg.append((1024 * j, 0, (WAR, 'war', 6144),
                               fqb_s, fqbs_s, RKC + 1024 * j, RKS + 1024 * j,
                               qt, qn, qoff + 512 * j))
                return tg

            def proj_serial(g, duo, KB, QB):
                dbias = 2 * g + duo
                for idx, (tok0, wA0, (wBt, wBn, wB0), bA, bB, c0, s0,
                          dt_, dn, f0) in enumerate(_targets(g, duo, KB, QB)):
                    pA, pAn = (ps[0], 0) if idx % 2 == 0 else (ps[2], 2)
                    pB, pBn = (ps[1], 1) if idx % 2 == 0 else (ps[3], 3)
                    for pb, pbn, wt, wn, w0 in ((pA, pAn, WAR, 'war', wA0),
                                                (pB, pBn, wBt, wBn, wB0)):
                        wr = RD(wn, 0, 128, w0, w0 + 2048)
                        for ic in range(8):
                            P.op('pe', lambda eng, o=pb[:],
                                 l=wt[:, w0 + 256 * ic + 128 * duo:
                                       w0 + 256 * ic + 128 * duo + 128],
                                 r=AB[ic][:, tok0:tok0 + 512],
                                 s=(ic == 0), t=(ic == 7):
                                 eng.matmul(o, l, r, start=s, stop=t),
                                 reads=[wr,
                                        RD(f'ab{ic}', 0, 128, tok0, tok0 + 512)],
                                 writes=[PSR(pbn)])
                    sc0 = 1024 * (idx % 2)
                    P.op('dve', lambda eng, o=SCR[:, sc0:sc0 + 512], i=pA[:],
                         b=bA[:, dbias:dbias + 1], t=ZAR[:, c0:c0 + 512]:
                         eng.scalar_tensor_tensor(o, i, b, t, ALU.add, ALU.mult),
                         reads=[PSR(pAn), ZR(c0, c0 + 512),
                                RD('cst', 0, 1, 0, 1)],
                         writes=[RD('scr', 0, 128, sc0, sc0 + 512)])
                    P.op('dve', lambda eng, o=SCR[:, sc0 + 512:sc0 + 1024],
                         i=pB[:], b=bB[:, dbias:dbias + 1],
                         t=ZAR[:, s0:s0 + 512]:
                         eng.scalar_tensor_tensor(o, i, b, t, ALU.add, ALU.mult),
                         reads=[PSR(pBn), ZR(s0, s0 + 512),
                                RD('cst', 0, 1, 0, 1)],
                         writes=[RD('scr', 0, 128, sc0 + 512, sc0 + 1024)])
                    P.op('dve', lambda eng, o=dt_[:, f0:f0 + 512],
                         i=SCR[:, sc0:sc0 + 512],
                         j2=SCR[:, sc0 + 512:sc0 + 1024]:
                         eng.tensor_add(o, i, j2),
                         reads=[RD('scr', 0, 128, sc0, sc0 + 1024)],
                         writes=[RD(dn, 0, 128, f0, f0 + 512)])

            def queue_proj(g, duo, KB, QB):
                dbias = 2 * g + duo
                for (tok0, wA0, (wBt, wBn, wB0), bA, bB, c0, s0,
                     dt_, dn, f0) in _targets(g, duo, KB, QB):
                    for wt, wn, w0, bcol, tab0, scdst in (
                            (WAR, 'war', wA0, bA, c0, 512),
                            (wBt, wBn, wB0, bB, s0, 1024)):
                        for ic0 in (0, 4):
                            def mmi(wt=wt, wn=wn, w0=w0, ic0=ic0, tok0=tok0,
                                    duo=duo):
                                wr = RD(wn, 0, 128, w0, w0 + 2048)
                                for ic in range(ic0, ic0 + 4):
                                    P.op('pe', lambda eng, o=ps[5][:],
                                         l=wt[:, w0 + 256 * ic + 128 * duo:
                                               w0 + 256 * ic + 128 * duo + 128],
                                         r=AB[ic][:, tok0:tok0 + 512],
                                         s=(ic == 0), t=(ic == 7):
                                         eng.matmul(o, l, r, start=s, stop=t),
                                         reads=[wr, RD(f'ab{ic}', 0, 128,
                                                       tok0, tok0 + 512)],
                                         writes=[PSR(5)])
                            filler.append(mmi)
                        def evi(bcol=bcol, tab0=tab0, scdst=scdst, dbias=dbias):
                            P.op('dve', lambda eng,
                                 o=SCR[:, scdst:scdst + 512], i=ps[5][:],
                                 b=bcol[:, dbias:dbias + 1],
                                 t=ZAR[:, tab0:tab0 + 512]:
                                 eng.scalar_tensor_tensor(o, i, b, t,
                                                          ALU.add, ALU.mult),
                                 reads=[PSR(5), ZR(tab0, tab0 + 512),
                                        RD('cst', 0, 1, 0, 1)],
                                 writes=[RD('scr', 0, 128, scdst, scdst + 512)])
                        filler.append(evi)
                    def addi(dt_=dt_, dn=dn, f0=f0):
                        P.op('dve', lambda eng, o=dt_[:, f0:f0 + 512],
                             i=SCR[:, 512:1024], j2=SCR[:, 1024:1536]:
                             eng.tensor_add(o, i, j2),
                             reads=[RD('scr', 0, 128, 512, 1536)],
                             writes=[RD(dn, 0, 128, f0, f0 + 512)])
                    filler.append(addi)

            def emit_fin(hg, j, par):
                avs = PSV[:, 512 * par:512 * par + 512]
                avrow = RD('psv', 64, 65, 512 * par, 512 * par + 512)
                avlow = RD('psv', 0, 64, 512 * par, 512 * par + 512)
                ao = 64 * hg % 128
                af = 2048 * (hg // 2) + 512 * j
                def fin():
                    P.op('dve', lambda eng, o=rows[0][:],
                         i=avs[64:65, :]: eng.reciprocal(o, i),
                         reads=[avrow], writes=[RD('row0', 0, 1, 0, 512)])
                    P.op('pe', lambda eng, o=ps[4][0:64, :],
                         l=ones1f_s[0:1, 0:64], r=rows[0][:]:
                         eng.matmul(o, l, r, start=True, stop=True),
                         reads=[RD('row0', 0, 1, 0, 512)] + CST,
                         writes=[PSR(4)])
                    P.op('dve', lambda eng, o=SCR[0:64, 0:512],
                         i=ps[4][0:64, :]: eng.tensor_copy(o, i),
                         reads=[PSR(4)], writes=[RD('scr', 0, 64, 0, 512)])
                    P.op('dve', lambda eng, o=MAT[ao:ao + 64, af:af + 512],
                         i=avs[0:64, :], j2=SCR[0:64, 0:512]:
                         eng.tensor_mul(o, i, j2),
                         reads=[avlow, RD('scr', 0, 64, 0, 512)],
                         writes=[RD('mat', ao, ao + 64, af, af + 512)])
                return fin

            def units(g, duo, KB, QB):
                kt, kn = KB
                qt, qn, qoff = QB
                tot_items = len(filler)
                slot = [0]
                done = [0]
                def paced_pump():
                    slot[0] += 1
                    want = tot_items * slot[0] // 128
                    while done[0] < want and filler:
                        filler.pop(0)()
                        done[0] += 1
                for hh in range(2):
                    hb = 64 * hh
                    hg = 2 * duo + hh
                    for j in range(4):
                        par = unit_ctr[0] % 2
                        unit_ctr[0] += 1
                        avs = PSV[:, 512 * par:512 * par + 512]
                        avr = RD('psv', 0, 128, 512 * par, 512 * par + 512)
                        qr = RD(qn, hb, hb + 64, qoff + 512 * j,
                                qoff + 512 * j + 512)
                        def sc_mm(c, hb=hb, j=j, qr=qr):
                            sl = PSA[:, 512 * (c % 4):512 * (c % 4) + 512]
                            slr = RD('psa', 0, 128, 512 * (c % 4),
                                     512 * (c % 4) + 512)
                            P.op('pe', lambda eng, o=sl,
                                 l=kt[hb:hb + 64, 128 * c:128 * c + 128],
                                 r=qt[hb:hb + 64, qoff + 512 * j:
                                      qoff + 512 * j + 512]:
                                 eng.matmul(o, l, r, start=True, stop=True),
                                 reads=[RD(kn, hb, hb + 64, 128 * c,
                                           128 * c + 128), qr],
                                 writes=[slr])
                        def av_mm(c, hg=hg, avs=avs, avr=avr):
                            col = 1024 * ((c // 2) % 2) + 512 * (c % 2)
                            P.op('pe', lambda eng, o=avs[0:65, :],
                                 l=VAR[:, 260 * c + 65 * hg:
                                        260 * c + 65 * hg + 65],
                                 r=ZAR[:, col:col + 512],
                                 s=(c == 0), t=(c == 31):
                                 eng.matmul(o, l, r, start=s, stop=t),
                                 reads=[RD('var', 0, 128, 260 * c + 65 * hg,
                                           260 * c + 65 * hg + 65),
                                        ZR(col, col + 512)],
                                 writes=[avr])
                        for gi in range(16):
                            sc_mm(2 * gi)
                            sc_mm(2 * gi + 1)
                            if gi >= 1:
                                av_mm(2 * gi - 2)
                                av_mm(2 * gi - 1)
                            if gi == 1 and pend_fin[0] is not None:
                                pend_fin[0]()
                                pend_fin[0] = None
                            paced_pump()
                            src0 = 1024 * (gi % 2)
                            P.op('act', lambda eng,
                                 o=ZAR[:, src0:src0 + 1024],
                                 i=PSA[:, src0:src0 + 1024]:
                                 eng.activation(o, i, AF.Exp, bias=0.0,
                                                scale=1.0),
                                 reads=[RD('psa', 0, 128, src0, src0 + 1024)],
                                 writes=[ZR(src0, src0 + 1024)])
                        av_mm(30)
                        av_mm(31)
                        pend_fin[0] = emit_fin(hg, j, par)

            def out_proj(g):
                for j in range(4):
                    if j == 3 and pend_fin[0] is not None:
                        pend_fin[0]()
                        pend_fin[0] = None
                    for oc in range(8):
                        ob_, obn = (ps[5], 5) if oc % 2 == 0 else (ps[1], 1)
                        for ic in range(2):
                            P.op('pe', lambda eng, o=ob_[:],
                                 l=ZAR[:, 10240 + 1024 * ic + 128 * oc:
                                        10240 + 1024 * ic + 128 * oc + 128],
                                 r=MAT[:, 2048 * ic + 512 * j:
                                        2048 * ic + 512 * j + 512],
                                 s=(ic == 0), t=(ic == 1):
                                 eng.matmul(o, l, r, start=s, stop=t),
                                 reads=[ZR(10240, 12288),
                                        RD('mat', 0, 128, 2048 * ic + 512 * j,
                                           2048 * ic + 512 * j + 512)],
                                 writes=[PSR(obn)])
                        P.op('dve', lambda eng,
                             o=FAR[:, 512 * oc:512 * oc + 512],
                             i=ob_[:]: eng.tensor_copy(o, i),
                             reads=[PSR(obn)],
                             writes=[RD('far', 0, 128, 512 * oc, 512 * oc + 512)])
                    P.dma('io', lambda eng, d=resd[:, 512 * j:512 * j + 512]
                          .rearrange("(ic p) t -> p ic t", p=128),
                          s=FAR[:].rearrange("p (ic t) -> p ic t", t=512):
                          eng.dma_start(out=d, in_=s, accum_op=ALU.add), 'x1acc',
                          reads=[RD('far', 0, 128, 0, 4096)],
                          writes=[RD('resd', 0, 128, 512 * j, 512 * j + 512)],
                          tag='x1acc')

            load_qkw(0)
            load_ow(0)
            v_loop(0)
            proj_serial(0, 0, KBUF[0], QBUF[0])
            for grp in range(4):
                queue_proj(grp, 1, KBUF[1], QBUF[1])
                units(grp, 0, KBUF[0], QBUF[0])
                drain()
                if grp < 3:
                    load_qkw(grp + 1)
                    queue_proj(grp + 1, 0, KBUF[0], QBUF[0])
                units(grp, 1, KBUF[1], QBUF[1])
                drain()
                out_proj(grp)
                if grp < 3:
                    load_ow(grp + 1)
                    v_loop(grp + 1)

        if _PHASES >= 4:
            P.dma('io', lambda eng, d=WAR[:, 4096:4160]
                  .rearrange("p (ic c) -> p ic c", c=8),
                  s=gwT.rearrange("(ic p) c -> p ic c", p=128):
                  eng.dma_start(out=d, in_=s), 'wd',
                  writes=[RD('war', 0, 128, 4096, 4160)])
            for tj in range(4):
                for hf in range(2):
                    pb = (2 * tj + hf) % 2
                    X1 = FAR[:, 2048 * pb:2048 * pb + 2048]
                    x1r = RD('far', 0, 128, 2048 * pb, 2048 * pb + 2048)
                    cols0 = 512 * tj + 256 * hf
                    P.dma('sp', lambda eng,
                          d=X1.rearrange("p (ic t) -> p ic t", t=256),
                          s=resd[:, cols0:cols0 + 256]
                          .rearrange("(ic p) t -> p ic t", p=128):
                          eng.dma_start(out=d, in_=s), f'x1l{pb}',
                          reads=[RD('resd', 0, 128, cols0, cols0 + 256)],
                          writes=[x1r], tag='x1l')
                    for ic in range(8):
                        P.op('act', lambda eng, o=bt[0][:, 0:256],
                             i=X1[:, 256 * ic:256 * ic + 256], b=fob_s[:, ic:ic + 1]:
                             eng.activation(o, i, AF.Square, bias=b, scale=1.0),
                             reads=[x1r, RD('cst', 0, 1, 0, 1)],
                             writes=[BTR(0)])
                        P.op('pe', lambda eng, o=ps[3][0:1, 0:256], l=onescol_s[:],
                             r=bt[0][:, 0:256], s=(ic == 0), t=(ic == 7):
                             eng.matmul(o, l, r, start=s, stop=t),
                             reads=[BTR(0)] + CST,
                             writes=[PSR(3)])
                    P.op('act', lambda eng, o=rows[0][0:1, 0:256],
                         i=ps[3][0:1, 0:256]:
                         eng.activation(o, i, AF.Ln, bias=fc_s[0:1, 1:2], scale=1.0 / 1024),
                         reads=[PSR(3)], writes=[RD('row0', 0, 1, 0, 512)])
                    P.op('act', lambda eng, o=rows[1][0:1, 0:256],
                         i=rows[0][0:1, 0:256]:
                         eng.activation(o, i, AF.Exp, bias=0.0, scale=-0.5),
                         reads=[RD('row0', 0, 1, 0, 512)],
                         writes=[RD('row1', 0, 1, 0, 512)])
                    P.op('pe', lambda eng, o=ps[4][:, 0:256], l=ones1f_s[:],
                         r=rows[1][0:1, 0:256]:
                         eng.matmul(o, l, r, start=True, stop=True),
                         reads=[RD('row1', 0, 1, 0, 512)] + CST, writes=[PSR(4)])
                    P.op('act', lambda eng, o=SCR[:, 512:768], s=ps[4][:, 0:256]:
                         eng.activation(o, s, AF.Identity, bias=0.0, scale=1.0),
                         reads=[PSR(4)], writes=[RD('scr', 0, 128, 512, 768)])
                    for ic in range(8):
                        dc = 2048 * ic + cols0
                        P.op('dve', lambda eng, o=bt[1][:, 0:256],
                             i=X1[:, 256 * ic:256 * ic + 256], b=fob_s[:, ic:ic + 1]:
                             eng.tensor_scalar(o, i, b, None, ALU.add),
                             reads=[x1r, RD('cst', 0, 1, 0, 1)],
                             writes=[BTR(1)])
                        P.op('dve', lambda eng, o=ZAR[:, dc:dc + 256],
                             i=bt[1][:, 0:256], w=n2w_s[:, ic:ic + 1],
                             r2=SCR[:, 512:768]:
                             eng.scalar_tensor_tensor(o, i, w, r2, ALU.mult, ALU.mult),
                             reads=[BTR(1),
                                    RD('scr', 0, 128, 512, 768),
                                    RD('cst', 0, 1, 0, 1)],
                             writes=[RD('zar', 0, 128, dc, dc + 256)])
            XN2 = lambda ic, a, b: RD('zar', 0, 128, 2048 * ic + a, 2048 * ic + b)
            for tt in range(16):
                for ic in range(8):
                    P.op('pe', lambda eng, o=ps[3][:, 0:8],
                         l=ZAR[:, 2048 * ic + 128 * tt:2048 * ic + 128 * tt + 128],
                         r=WAR[:, 4096 + 8 * ic:4096 + 8 * ic + 8],
                         s=(ic == 0), t=(ic == 7): eng.matmul(o, l, r, start=s, stop=t),
                         reads=[XN2(ic, 128 * tt, 128 * tt + 128),
                                RD('war', 0, 128, 4096, 4160)],
                         writes=[PSR(3)])
                _gate_top2(P, ps, ps7, lgt, m1c, m2c, i1c, i2c, t8c, combc,
                           cpT, idf_s, tt, CST)
            for e in range(E):
                P.dma('sp', lambda eng, d=SCR[:, 0:16], s=b1l[e]:
                      eng.dma_start(out=d, in_=s), 'bexp',
                      writes=[RD('scr', 0, 128, 0, 16)])
                P.dma('sp', lambda eng, d=SCR[:, 16:32], s=b2l[e]:
                      eng.dma_start(out=d, in_=s), 'bexp',
                      writes=[RD('scr', 0, 128, 16, 32)])
                P.dma('sp', lambda eng, d=SCR[:, 32:40], s=b3l[e]:
                      eng.dma_start(out=d, in_=s), 'bexp',
                      writes=[RD('scr', 0, 128, 32, 40)])
                for k in range(16):
                    pb = k % 2
                    w1o, w2o = 1024 * pb, 2048 + 1024 * pb
                    w1w = RD('war', 0, 128, w1o, w1o + 1024)
                    w2w = RD('war', 0, 128, w2o, w2o + 1024)
                    w1s = WAR[:, w1o:w1o + 1024]
                    w2s = WAR[:, w2o:w2o + 1024]
                    P.dma('io', lambda eng,
                          d=w1s.rearrange("p (ic c) -> p ic c", c=128),
                          s=w1p[e][k]:
                          eng.dma_start(out=d, in_=s), f'w1l{pb}', writes=[w1w], tag='w1l')
                    P.dma('io', lambda eng,
                          d=w2s.rearrange("p (ic c) -> p ic c", c=128),
                          s=w2p[e][k]:
                          eng.dma_start(out=d, in_=s), f'w2l{pb}', writes=[w2w], tag='w2l')
                    hcol = 2048 * (k % 2)
                    hab = k // 2
                    for cb2 in range(4):
                        fcc = 512 * cb2
                        ub, ubn = (ps[0], 0) if cb2 % 2 == 0 else (ps[2], 2)
                        vb2, vb2n = (ps[1], 1) if cb2 % 2 == 0 else (ps[4], 4)
                        bs, bsn = (bt[0], bt[1]) if cb2 % 2 == 0 else (bt[2], bt[3])
                        ob0 = 0 if cb2 % 2 == 0 else 1024
                        for ic in range(8):
                            P.op('pe', lambda eng, o=ub[:],
                                 l=w1s[:, 128 * ic:128 * ic + 128],
                                 r=ZAR[:, 2048 * ic + fcc:2048 * ic + fcc + 512],
                                 s=(ic == 0), t=(ic == 7):
                                 eng.matmul(o, l, r, start=s, stop=t),
                                 reads=[w1w, XN2(ic, fcc, fcc + 512)],
                                 writes=[PSR(ubn)])
                        for ic in range(8):
                            P.op('pe', lambda eng, o=vb2[:],
                                 l=w2s[:, 128 * ic:128 * ic + 128],
                                 r=ZAR[:, 2048 * ic + fcc:2048 * ic + fcc + 512],
                                 s=(ic == 0), t=(ic == 7):
                                 eng.matmul(o, l, r, start=s, stop=t),
                                 reads=[w2w, XN2(ic, fcc, fcc + 512)],
                                 writes=[PSR(vb2n)])
                        P.op('act', lambda eng, o=bs[:], i=ub[:],
                             b=SCR[:, 0 + k:0 + k + 1]:
                             eng.activation(o, i, AF.Silu, bias=b, scale=1.0),
                             reads=[PSR(ubn), RD('scr', 0, 128, 0, 16)],
                             writes=[RD('bt2c', 0, 128, ob0, ob0 + 512)])
                        P.op('dve', lambda eng, o=bsn[:], i=vb2[:],
                             b=SCR[:, 16 + k:16 + k + 1]:
                             eng.tensor_scalar(o, i, b, None, ALU.add),
                             reads=[PSR(vb2n), RD('scr', 0, 128, 16, 32)],
                             writes=[RD('bt2c', 0, 128, ob0 + 512, ob0 + 1024)])
                        P.op('dve', lambda eng,
                             o=AB[hab][:, hcol + fcc:hcol + fcc + 512],
                             i=bs[:], j2=bsn[:]: eng.tensor_mul(o, i, j2),
                             reads=[RD('bt2c', 0, 128, ob0, ob0 + 1024)],
                             writes=[RD(f'ab{hab}', 0, 128, hcol + fcc,
                                        hcol + fcc + 512)])
                def w3slot(k):
                    if k < 8:
                        return VAR[:, 1024 * k:1024 * k + 1024], \
                            RD('var', 0, 128, 1024 * k, 1024 * k + 1024)
                    if k < 12:
                        return KAR[:, 1024 * (k - 8):1024 * (k - 8) + 1024], \
                            RD('kar', 0, 128, 1024 * (k - 8), 1024 * (k - 8) + 1024)
                    return MAT[:, 1024 * (k - 12):1024 * (k - 12) + 1024], \
                        RD('mat', 0, 128, 1024 * (k - 12), 1024 * (k - 12) + 1024)
                for k in range(16):
                    sl, slr = w3slot(k)
                    P.dma('sp', lambda eng, d=sl,
                          s=w3bf[2048 * e + 128 * k:2048 * e + 128 * k + 128, :]:
                          eng.dma_start(out=d, in_=s), 'w3l',
                          reads=[RD('w3bf', 0, 128, 2048 * e, 2048 * e + 2048)],
                          writes=[slr], tag='w3l')
                for tj in range(4):
                    P.op('pe', lambda eng, o=ps[5][:],
                         l=onehots_s[:, 128 * e:128 * e + 128],
                         r=cpT[0:8, 512 * tj:512 * tj + 512]:
                         eng.matmul(o, l, r, start=True, stop=True),
                         reads=[RD('cpT', 0, 8, 512 * tj, 512 * tj + 512)] + CST,
                         writes=[PSR(5)])
                    P.op('act', lambda eng, o=bt[4][:], s=ps[5][:]:
                         eng.activation(o, s, AF.Identity, bias=0.0, scale=1.0),
                         reads=[PSR(5)], writes=[RD('bt4', 0, 128, 0, 512)])
                    for ocg in range(2):
                        bks = ([(ps[0], 0), (ps[1], 1), (ps[2], 2), (ps[4], 4)]
                               if ocg == 0 else
                               [(ps[3], 3), (ps[5], 5), (ps7, 7), (ps[1], 1)])
                        for k in range(16):
                            sl, slr = w3slot(k)
                            for o4 in range(4):
                                oc = 4 * ocg + o4
                                bk, bn = bks[o4]
                                P.op('pe', lambda eng, o=bk[:],
                                     l=sl[:, 128 * oc:128 * oc + 128],
                                     r=AB[k // 2][:, 2048 * (k % 2) + 512 * tj:
                                                   2048 * (k % 2) + 512 * tj + 512],
                                     s=(k == 0), t=(k == 15):
                                     eng.matmul(o, l, r, start=s, stop=t),
                                     reads=[slr,
                                            RD(f'ab{k // 2}', 0, 128,
                                               2048 * (k % 2) + 512 * tj,
                                               2048 * (k % 2) + 512 * tj + 512)],
                                     writes=[PSR(bn)])
                        for o4 in range(4):
                            oc = 4 * ocg + o4
                            bk, bn = bks[o4]
                            P.op('dve', lambda eng,
                                 o=FAR[:, 512 * oc:512 * oc + 512], i=bk[:],
                                 b=SCR[:, 32 + oc:32 + oc + 1], cb=bt[4][:]:
                                 eng.scalar_tensor_tensor(o, i, b, cb,
                                                          ALU.add, ALU.mult),
                                 reads=[PSR(bn), RD('bt4', 0, 128, 0, 512),
                                        RD('scr', 0, 128, 32, 40)],
                                 writes=[RD('far', 0, 128, 512 * oc, 512 * oc + 512)])
                    P.dma('io', lambda eng, d=resd[:, 512 * tj:512 * tj + 512]
                          .rearrange("(ic p) t -> p ic t", p=128),
                          s=FAR[:].rearrange("p (ic t) -> p ic t", t=512):
                          eng.dma_start(out=d, in_=s, accum_op=ALU.add), 'x1acc',
                          reads=[RD('far', 0, 128, 0, 4096)],
                          writes=[RD('resd', 0, 128, 512 * tj, 512 * tj + 512)],
                          tag='yacc')

        if _PHASES >= 5:
            for om in range(16):
                j, sub = om // 4, om % 4
                pb = om % 2
                x2t = FAR[:, 0:1024]
                x2r = RD('far', 0, 128, 0, 1024)
                stt = FAR[:, 1024:2048]
                str_ = RD('far', 0, 128, 1024, 2048)
                xt = FAR[:, 2048:3072]
                xtr = RD('far', 0, 128, 2048, 3072)
                ot = FAR[:, 3072:4096]
                otr = RD('far', 0, 128, 3072, 4096)
                P.dma('sp', lambda eng, d=x2t.rearrange("p (ic t) -> p ic t", t=128),
                      s=resd[:, 128 * om:128 * om + 128]
                      .rearrange("(ic p) t -> p ic t", p=128):
                      eng.dma_start(out=d, in_=s), 'x2l',
                      reads=[RD('resd', 0, 128, 128 * om, 128 * om + 128)],
                      writes=[x2r], tag='x2l')
                P.dma('sp', lambda eng, d=stt.rearrange("p (ic t) -> p ic t", t=128),
                      s=sd[:, 128 * om:128 * om + 128]
                      .rearrange("(ic p) t -> p ic t", p=128):
                      eng.dma_start(out=d, in_=s), 'sl',
                      reads=[RD('sd', 0, 128, 128 * om, 128 * om + 128)],
                      writes=[str_], tag='sl')
                xrow = 1024 * j + 128 * sub
                P.dma('sp', lambda eng, d=xt, s=xp[xrow:xrow + 128, :]:
                      eng.dma_start(out=d, in_=s), 'xl2', writes=[xtr], tag='xl2')
                for ic in range(8):
                    P.op('act', lambda eng, o=SCR[:, 128 * ic:128 * ic + 128],
                         i=x2t[:, 128 * ic:128 * ic + 128], b=fob_s[:, ic:ic + 1]:
                         eng.activation(o, i, AF.Silu, bias=b, scale=1.0),
                         reads=[x2r, RD('cst', 0, 1, 0, 1)],
                         writes=[RD('scr', 0, 128, 128 * ic, 128 * ic + 128)])
                for ic in range(8):
                    P.op('dve', lambda eng, o=SCR[:, 1024 + 128 * ic:1152 + 128 * ic],
                         i=stt[:, 128 * ic:128 * ic + 128], b=bo_s[:, ic:ic + 1],
                         g2=SCR[:, 128 * ic:128 * ic + 128]:
                         eng.scalar_tensor_tensor(o, i, b, g2, ALU.add, ALU.mult),
                         reads=[str_, RD('scr', 0, 128, 128 * ic, 128 * ic + 128),
                                RD('cst', 0, 1, 0, 1)],
                         writes=[RD('scr', 0, 128, 1024 + 128 * ic, 1152 + 128 * ic)])
                for ic in range(8):
                    bk, bn = (ps[0], 0) if ic < 4 else (ps[1], 1)
                    fo = 128 * (ic % 4)
                    P.op('pe', lambda eng, o=bk[:, fo:fo + 128],
                         a=SCR[:, 1024 + 128 * ic:1152 + 128 * ic], idn=idf_s[:]:
                         eng.transpose(o, a, idn),
                         reads=[RD('scr', 0, 128, 1024 + 128 * ic, 1152 + 128 * ic)]
                         + CST,
                         writes=[PSR(bn)])
                for half in range(2):
                    bk, bn = (ps[0], 0) if half == 0 else (ps[1], 1)
                    P.op('dve', lambda eng, o=ot[:, 512 * half:512 * half + 512],
                         i=bk[:], j2=xt[:, 512 * half:512 * half + 512]:
                         eng.tensor_add(o, i, j2),
                         reads=[PSR(bn), xtr], writes=[otr])
                P.dma('act', lambda eng, d=out_d[128 * om:128 * om + 128, :], s=ot:
                      eng.dma_start(out=d, in_=s), 'out',
                      reads=[otr], writes=[RD('outd', 0, 128, 0, 1024)], tag='out')

        P.finalize(nc, st)
    return nc


def _gate_top2(P, ps, ps7, lgt, m1c, m2c, i1c, i2c, t8c, combc, cpT, idf_s, tt, CST):
    PSR3 = RD('ps3', 0, 128, 0, 512)
    PS7R = RD('ps7', 0, 128, 0, 512)
    P.op('dve', lambda eng, o=lgt[:], i=ps[3][:, 0:8]: eng.tensor_copy(o, i),
         reads=[PSR3], writes=[RD('lgt', 0, 128, 0, 8)])
    P.op('dve', lambda eng, o=m1c[:], i=lgt[:]: eng.reduce_max(o, i, AX.X),
         reads=[RD('lgt', 0, 128, 0, 8)], writes=[RD('m1c', 0, 128, 0, 1)])
    P.op('dve', lambda eng, o=i1c[:], i=lgt[:], m=m1c[:]:
         eng.tensor_scalar(o, i, m, None, ALU.is_ge),
         reads=[RD('lgt', 0, 128, 0, 8), RD('m1c', 0, 128, 0, 1)],
         writes=[RD('i1c', 0, 128, 0, 8)])
    P.op('dve', lambda eng, o=t8c[:], i=i1c[:], j2=lgt[:]:
         eng.scalar_tensor_tensor(o, i, -1e9, j2, ALU.mult, ALU.add),
         reads=[RD('i1c', 0, 128, 0, 8), RD('lgt', 0, 128, 0, 8)],
         writes=[RD('t8c', 0, 128, 0, 8)])
    P.op('dve', lambda eng, o=m2c[:], i=t8c[:]: eng.reduce_max(o, i, AX.X),
         reads=[RD('t8c', 0, 128, 0, 8)], writes=[RD('m2c', 0, 128, 0, 1)])
    P.op('dve', lambda eng, o=i2c[:], i=t8c[:], m=m2c[:]:
         eng.tensor_scalar(o, i, m, None, ALU.is_ge),
         reads=[RD('t8c', 0, 128, 0, 8), RD('m2c', 0, 128, 0, 1)],
         writes=[RD('i2c', 0, 128, 0, 8)])
    P.op('dve', lambda eng, o=m1c[:], i=m1c[:], j2=m2c[:]: eng.tensor_sub(o, i, j2),
         reads=[RD('m1c', 0, 128, 0, 1), RD('m2c', 0, 128, 0, 1)],
         writes=[RD('m1c', 0, 128, 0, 1)])
    P.op('act', lambda eng, o=m1c[:], i=m1c[:]:
         eng.activation(o, i, AF.Sigmoid, bias=0.0, scale=1.0),
         reads=[RD('m1c', 0, 128, 0, 1)], writes=[RD('m1c', 0, 128, 0, 1)])
    P.op('dve', lambda eng, o=combc[:], i=i1c[:], m=m1c[:]:
         eng.tensor_scalar(o, i, m, None, ALU.mult),
         reads=[RD('i1c', 0, 128, 0, 8), RD('m1c', 0, 128, 0, 1)],
         writes=[RD('combc', 0, 128, 0, 8)])
    P.op('dve', lambda eng, o=t8c[:], i=i2c[:], m=m1c[:]:
         eng.tensor_scalar(o, i, m, -1.0, ALU.mult, ALU.mult),
         reads=[RD('i2c', 0, 128, 0, 8), RD('m1c', 0, 128, 0, 1)],
         writes=[RD('t8c', 0, 128, 0, 8)])
    P.op('dve', lambda eng, o=combc[:], i=combc[:], j2=t8c[:]:
         eng.tensor_add(o, i, j2),
         reads=[RD('combc', 0, 128, 0, 8), RD('t8c', 0, 128, 0, 8)],
         writes=[RD('combc', 0, 128, 0, 8)])
    P.op('dve', lambda eng, o=combc[:], i=combc[:], j2=i2c[:]:
         eng.tensor_add(o, i, j2),
         reads=[RD('combc', 0, 128, 0, 8), RD('i2c', 0, 128, 0, 8)],
         writes=[RD('combc', 0, 128, 0, 8)])
    P.op('pe', lambda eng, o=ps7[0:8, 0:128], a=combc[:], idn=idf_s[:]:
         eng.transpose(o, a, idn),
         reads=[RD('combc', 0, 128, 0, 8)] + CST, writes=[PS7R])
    P.op('dve', lambda eng, o=cpT[0:8, 128 * tt:128 * tt + 128],
         i=ps7[0:8, 0:128]: eng.tensor_copy(o, i),
         reads=[PS7R], writes=[RD('cpT', 0, 8, 128 * tt, 128 * tt + 128)])



_CACHE = {}

def _get_nc():
    if 'nc' not in _CACHE:
        nc = bass.Bass("TRN2", target_bir_lowering=False)
        _build(nc)
        _CACHE['nc'] = nc
    return _CACHE['nc']


def _prep_shared(d):
    ac = np.ascontiguousarray
    Wqk = np.asarray(d['L0_attn_W_qk'])
    W3 = Wqk.reshape(8, 128, 2048)
    q4 = W3[:, :, :1024].reshape(8, 128, 8, 128).transpose(2, 1, 0, 3)
    k4 = W3[:, :, 1024:].reshape(8, 128, 8, 128).transpose(2, 1, 0, 3)
    qkvT = np.asarray(d['L0_ffn_qkv_w']).T
    def _swapcols(W):
        W4 = W.reshape(-1, 16, 2, 32)
        return np.stack([-W4[:, :, 1], W4[:, :, 0]], axis=2).reshape(-1, 1024)
    qkvT4 = np.concatenate([qkvT, _swapcols(qkvT[:, :1024])], axis=1)
    ksw = _swapcols(qkvT[:, 1024:2048])
    def _swaprows(Bm):
        B4 = Bm.reshape(2, 2, 32, 8)
        return np.ascontiguousarray(
            np.stack([-B4[:, 1], B4[:, 0]], axis=1).reshape(128, 8))
    outwT = np.asarray(d['L0_ffn_out_w']).T
    w1T = np.asarray(d['L0_ffn_w1']).transpose(0, 2, 1)
    w2T = np.asarray(d['L0_ffn_w2']).transpose(0, 2, 1)
    shared = dict(
        wqkp=ac(np.concatenate([q4, k4], axis=3)),
        wvp=ac(np.asarray(d['L0_attn_W_v']).reshape(8, 128, 8, 256).transpose(2, 1, 0, 3)),
        wgp=ac(np.asarray(d['L0_attn_W_g']).reshape(8, 128, 8, 256).transpose(2, 1, 0, 3)),
        wop=ac(np.asarray(d['L0_attn_W_o']).reshape(8, 2, 128, 1024).transpose(0, 2, 1, 3)),
        fqp=ac(qkvT4.reshape(8, 128, 4, 4, 256).transpose(3, 1, 2, 0, 4)),
        fksp=ac(ksw.reshape(8, 128, 4, 256).transpose(2, 1, 0, 3)),
        fop=ac(outwT.reshape(4, 2, 128, 1024).transpose(0, 2, 1, 3)),
        w1p=ac(w1T.reshape(E, 8, 128, 16, 128).transpose(0, 3, 2, 1, 4)),
        w2p=ac(w2T.reshape(E, 8, 128, 16, 128).transpose(0, 3, 2, 1, 4)),
        bqkq=ac(np.asarray(d['L0_attn_b_qk'])[:1024].reshape(8, 128).T),
        bqkk=ac(np.asarray(d['L0_attn_b_qk'])[1024:].reshape(8, 128).T),
        bvrow=ac(np.asarray(d['L0_attn_b_v'])),
        bg=ac(np.asarray(d['L0_attn_b_g']).reshape(16, 128).T),
        bo=ac(np.asarray(d['L0_attn_b_o']).reshape(8, 128).T),
        fqb=ac(np.asarray(d['L0_ffn_qkv_b'])[:1024].reshape(8, 128).T),
        fkb=ac(np.asarray(d['L0_ffn_qkv_b'])[1024:2048].reshape(8, 128).T),
        fqbs=_swaprows(np.asarray(d['L0_ffn_qkv_b'])[:1024].reshape(8, 128).T),
        fkbs=_swaprows(np.asarray(d['L0_ffn_qkv_b'])[1024:2048].reshape(8, 128).T),
        fvb=ac(np.asarray(d['L0_ffn_qkv_b'])[2048:]),
        fob=ac(np.asarray(d['L0_ffn_out_b']).reshape(8, 128).T),
        n1w=ac(np.asarray(d['L0_ffn_norm1_w']).reshape(8, 128).T),
        n2w=ac(np.asarray(d['L0_ffn_norm2_w']).reshape(8, 128).T),
        gwT=ac(np.asarray(d['L0_ffn_gate_w']).T),
        w3T=ac(np.asarray(d['L0_ffn_w3']).transpose(0, 2, 1)),
        b1l=ac(np.asarray(d['L0_ffn_b1']).reshape(E, 16, 128).transpose(0, 2, 1)),
        b2l=ac(np.asarray(d['L0_ffn_b2']).reshape(E, 16, 128).transpose(0, 2, 1)),
        b3l=ac(np.asarray(d['L0_ffn_b3']).reshape(E, 8, 128).transpose(0, 2, 1)),
    )
    shared.update(_consts())
    return shared


def _in_map(shared, tabs, x, core):
    b, half = core // 2, core % 2
    G = _gperm(half)
    m = dict(shared)
    m.update(tabs[half])
    m['xp'] = np.ascontiguousarray(x[b].reshape(8, 512, C)[G].reshape(T, C))
    return m


def kernel(**inputs):
    x = np.asarray(inputs['L0_attn_x'])
    shared = _prep_shared(inputs)
    tabs = [_host_tables(0), _host_tables(1)]
    in_maps = [_in_map(shared, tabs, x, c) for c in range(8)]
    nc = _get_nc()
    res = run_bass_kernel_spmd(nc, in_maps, core_ids=list(range(8)))
    out = np.zeros((B, T, C), np.float32)
    for c in range(8):
        b, half = c // 2, c % 2
        G = _gperm(half)
        oc = res.results[c]['out']
        for j in range(OWN):
            gt = G[2 * j]
            out[b, 512 * gt:512 * gt + 512] = oc[512 * j:512 * j + 512]
    return out


# version 16
# speedup vs baseline: 1.0528x; 1.0528x over previous
import contextlib
import numpy as np
import ml_dtypes
import concourse.bass as bass
import concourse.mybir as mybir
from concourse.bass_utils import run_bass_kernel_spmd

F32 = mybir.dt.float32
BF16 = mybir.dt.bfloat16
AF = mybir.ActivationFunctionType
ALU = mybir.AluOpType
AX = mybir.AxisListType

B, T, C = 4, 4096, 1024
H, KD, VD, VBD = 16, 64, 128, 2048
FD = 64
E = 8
OWN = 4
EPS_RET = float(np.finfo(np.float32).eps)
LN_EPS = 1e-5
RMS_EPS = 1e-6
SM_SCALE = FD ** -0.5

_PHASES = 5
_B_PAIRS = 8
_B_SACC = True
_B_STAGE = 9
_B_S2 = 9
_DBG = False


def _gperm(half):
    return [p ^ 1 if half else p for p in range(8)]

def _host_tables(half):
    G = _gperm(half)
    gam = np.log1p(-np.power(2.0, -5.0 - np.arange(H, dtype=np.float64)))
    t64 = np.arange(T, dtype=np.float64)
    r = np.exp(gam)[:, None]
    scale = np.sqrt((1.0 - r ** (t64[None, :] + 1.0)) / (1.0 - r))
    qsc = np.zeros((H, OWN, 512), np.float64)
    for j in range(OWN):
        t0 = 512 * G[2 * j]
        qsc[:, j, :] = (np.exp(gam[:, None] * np.arange(512)[None, :])
                        / scale[:, t0:t0 + 512])
    ksc = np.exp(-gam[:, None] * (np.arange(512)[None, :] % 128))
    Ct = np.zeros((H, OWN, 32), np.float64)
    for j in range(OWN):
        t0 = 512 * G[2 * j]
        for c in range(8 * (j + 1)):
            s0 = 512 * G[c // 4] + 128 * (c % 4)
            with np.errstate(under='ignore'):
                Ct[:, j, c] = np.exp(gam * (t0 - s0))
    Ctab = np.tile(Ct.reshape(1, -1).astype(np.float32), (128, 1))
    mask8 = np.zeros((OWN, 8, 128, 512), np.float32)
    sr = np.arange(128)[:, None]
    tr = np.arange(512)[None, :]
    for j in range(OWN):
        t0 = 512 * G[2 * j]
        for lc in range(8):
            c = 8 * j + lc
            s0 = 512 * G[c // 4] + 128 * (c % 4)
            mask8[j, lc] = ((t0 + tr) >= (s0 + sr)).astype(np.float32)
    inv = 1.0 / 10000.0 ** (np.arange(0, FD, 2, dtype=np.float64) / FD)
    invf = np.concatenate([inv, inv])
    sq = np.sqrt(SM_SCALE)
    tq = np.concatenate([512 * G[2 * j] + np.arange(512) for j in range(OWN)])
    tk = np.concatenate([512 * G[p] + np.arange(512) for p in range(8)])
    bf = ml_dtypes.bfloat16
    return dict(
        qscd=qsc.reshape(H * OWN, 512).astype(bf),
        kscd=ksc.astype(bf),
        Ctab=np.ascontiguousarray(Ctab),
        mask8=mask8.astype(bf),
        rkc=_dup(np.cos(invf[:, None] * tk[None, :]) * sq).astype(bf),
        rks=_dup(np.sin(invf[:, None] * tk[None, :]) * sq).astype(bf),
    )

def _dup(t):
    return np.concatenate([t, t], axis=0)

def _consts():
    bf = ml_dtypes.bfloat16
    oh = np.zeros((8, 1024), np.float32)
    for e in range(8):
        oh[e, 128 * e:128 * e + 128] = 1.0
    return dict(
        fconst=np.tile(np.array([[LN_EPS, RMS_EPS, 0.0, 0.0]], np.float32), (128, 1)),
        idf=np.eye(128, dtype=np.float32),
        idb=np.eye(128, dtype=np.float32).astype(bf),
        onescol_bf=np.ones((128, 1), np.float32).astype(bf),
        ones1_bf=np.ones((1, 128), np.float32).astype(bf),
        ones1_f=np.ones((1, 128), np.float32),
        onehots=oh.astype(bf),
    )


class Prog:
    ENGS = ('pe', 'act', 'dve', 'sp', 'io')

    def __init__(self):
        self.ops = {e: [] for e in self.ENGS}
        self.w = {}
        self.r = {}
        self.dma_cnt = {}

    @staticmethod
    def _ov(a, b):
        return not (a[1] <= b[0] or b[1] <= a[0] or a[3] <= b[2] or b[3] <= a[2])

    def _deps(self, reads, writes, e=None):
        deps = []
        for (t, rect) in reads:
            for (r2, n) in self.w.get(t, ()):
                if self._ov(rect, r2):
                    deps.append(n)
            if t.startswith('ps') and e is not None:
                for (r2, n) in self.r.get(t, ()):
                    if n[0] != 'd' and n[0] != e and self._ov(rect, r2):
                        deps.append(n)
        for (t, rect) in writes:
            for (r2, n) in self.w.get(t, ()):
                if self._ov(rect, r2):
                    deps.append(n)
            for (r2, n) in self.r.get(t, ()):
                if self._ov(rect, r2):
                    deps.append(n)
        return deps

    def _record(self, reads, writes, node):
        for (t, rect) in reads:
            self.r.setdefault(t, []).append((rect, node))
        for (t, rect) in writes:
            def keep(entry):
                r2 = entry[0]
                return not (r2[0] >= rect[0] and r2[1] <= rect[1]
                            and r2[2] >= rect[2] and r2[3] <= rect[3])
            lw = self.w.setdefault(t, [])
            lw[:] = [en for en in lw if keep(en)]
            lw.append((rect, node))
            lr = self.r.get(t)
            if lr:
                lr[:] = [en for en in lr if keep(en)]

    def _norm(self, e, idx, deps):
        be, bd = {}, {}
        for d in deps:
            if d[0] == 'd':
                bd[d[1]] = self.dma_cnt[d[1]]
            else:
                de, di = d
                if de == 'pe' and e == 'pe':
                    continue
                if de == e and di >= idx:
                    raise RuntimeError('self-dep forward')
                if be.get(de, -1) < di:
                    be[de] = di
        return be, bd

    def op(self, e, emit, reads=(), writes=(), tag=''):
        idx = len(self.ops[e])
        be, bd = self._norm(e, idx, self._deps(reads, writes, e))
        if 'cst' in self.dma_cnt:
            bd.setdefault('cst', self.dma_cnt['cst'])
        self.ops[e].append(dict(emit=emit, be=be, bd=bd, inc=False,
                                dma=None, tag=tag))
        self._record(reads, writes, (e, idx))
        return (e, idx)

    def dma(self, q, emit, sem, reads=(), writes=(), tag=''):
        idx = len(self.ops[q])
        be, bd = self._norm(q, idx, self._deps(reads, writes))
        self.dma_cnt[sem] = self.dma_cnt.get(sem, 0) + 1
        self.ops[q].append(dict(emit=emit, be=be, bd=bd, inc=False,
                                dma=sem, tag=tag))
        self._record(reads, writes, ('d', sem))

    def finalize(self, nc, stack):
        for e in self.ENGS:
            for o in self.ops[e]:
                for de, di in o['be'].items():
                    self.ops[de][di]['inc'] = True
        pref = {}
        for e in self.ENGS:
            c = 0
            pl = []
            for o in self.ops[e]:
                if o['inc']:
                    c += 1
                pl.append(c)
            pref[e] = pl
        esem = {e: stack.enter_context(nc.semaphore(f'es_{e}'))
                for e in self.ENGS if any(o['inc'] for o in self.ops[e])}
        dsem = {s: stack.enter_context(nc.semaphore(f'ds_{s}'))
                for s in self.dma_cnt}
        block = stack.enter_context(nc.Block())

        def mk(e):
            ops = self.ops[e]
            def body(eng):
                water = {}
                for o in ops:
                    for de in sorted(o['be']):
                        val = pref[de][o['be'][de]]
                        if water.get(('e', de), 0) < val:
                            eng.wait_ge(esem[de], val)
                            water[('e', de)] = val
                    for s in sorted(o['bd']):
                        val = 16 * o['bd'][s]
                        if water.get(('d', s), 0) < val:
                            eng.wait_ge(dsem[s], val)
                            water[('d', s)] = val
                    inst = o['emit'](eng)
                    if o['dma'] is not None:
                        inst.then_inc(dsem[o['dma']], 16)
                    elif o['inc']:
                        inst.then_inc(esem[e], 1)
                if e == 'act' and 'out' in dsem:
                    eng.wait_ge(dsem['out'], 16 * self.dma_cnt['out'])
            return body

        if self.ops['pe']:
            block.tensor(mk('pe'))
        if self.ops['act']:
            block.scalar(mk('act'))
        if self.ops['dve']:
            block.vector(mk('dve'))
        if self.ops['sp']:
            block.sync(mk('sp'))
        if self.ops['io']:
            block.gpsimd(mk('io'))


def RD(t, p0, p1, f0, f1):
    return (t, (p0, p1, f0, f1))

ALLR = (0, 128, 0, 1 << 30)


def _build(nc):
    din = lambda n, s, dt=F32: nc.dram_tensor(n, s, dt, kind="ExternalInput").ap()
    xp = din('xp', [T, C])
    wqkp = din('wqkp', [8, 128, 8, 256]); wvp = din('wvp', [8, 128, 8, 256])
    wgp = din('wgp', [8, 128, 8, 256]); wop = din('wop', [8, 128, 2, 1024])
    bqkq = din('bqkq', [128, 8]); bqkk = din('bqkk', [128, 8])
    bvrow = din('bvrow', [VBD]); bg = din('bg', [128, H]); bo = din('bo', [128, 8])
    Ctab_d = din('Ctab', [128, 2048])
    qscd = din('qscd', [H * OWN, 512], BF16); kscd = din('kscd', [H, 512], BF16)
    mask8_d = din('mask8', [OWN, 8, 128, 512], BF16)
    fqp = din('fqp', [4, 128, 4, 8, 256]); fop = din('fop', [4, 128, 2, 1024])
    fksp = din('fksp', [4, 128, 8, 256])
    fqb = din('fqb', [128, 8]); fkb = din('fkb', [128, 8])
    fqbs = din('fqbs', [128, 8]); fkbs = din('fkbs', [128, 8])
    fvb = din('fvb', [C]); fob = din('fob', [128, 8])
    n1w = din('n1w', [128, 8]); n2w = din('n2w', [128, 8])
    rkc = din('rkc', [128, T], BF16); rks = din('rks', [128, T], BF16)
    gwT = din('gwT', [C, E])
    w1p = din('w1p', [E, 16, 128, 8, 128]); w2p = din('w2p', [E, 16, 128, 8, 128])
    w3T = din('w3T', [E, 2048, C])
    b1l = din('b1l', [E, 128, 16]); b2l = din('b2l', [E, 128, 16])
    b3l = din('b3l', [E, 128, 8])
    idf = din('idf', [128, 128]); idb = din('idb', [128, 128], BF16)
    fconst = din('fconst', [128, 4])
    onescol = din('onescol_bf', [128, 1], BF16)
    ones1b = din('ones1_bf', [1, 128], BF16)
    ones1f = din('ones1_f', [1, 128])
    onehots_d = din('onehots', [8, 1024], BF16)

    out_d = nc.dram_tensor('out', [2048, C], F32, kind="ExternalOutput").ap()
    skind = "ExternalOutput" if _DBG else "Internal"
    xnr1d = nc.dram_tensor('xnr1d', [C, T], BF16, kind=skind).ap()
    resd = nc.dram_tensor('resd', [C, 2048], F32, kind=skind).ap()
    sd = nc.dram_tensor('sd', [C, 2048], F32, kind=skind).ap()
    w3bf = nc.dram_tensor('w3bf', [E * 2048, C], BF16, kind="Internal").ap()

    st = contextlib.ExitStack()
    P = Prog()
    with st:
        def sb(name, shape, dt):
            return st.enter_context(nc.sbuf_tensor(name, shape, dt))
        AB = [sb(f'ab{i}', [128, T], BF16) for i in range(8)]
        ZAR = sb('zar', [128, 16384], BF16)
        VAR = sb('var', [128, 8320], BF16)
        WAR = sb('war', [128, 8192], BF16)
        FAR = sb('far', [128, 4096], F32)
        SCR = sb('scr', [128, 2048], F32)
        QAR = sb('qar', [128, 2048], BF16)
        KAR = sb('kar', [128, 4096], BF16)
        MAT = sb('mat', [128, 4096], BF16)
        idf_s = sb('idf_s', [128, 128], F32)
        idb_s = sb('idb_s', [128, 128], BF16)
        onescol_s = sb('onescol_s', [128, 1], BF16)
        ones1b_s = sb('ones1b_s', [1, 128], BF16)
        ones1f_s = sb('ones1f_s', [1, 128], F32)
        onehots_s = sb('onehots_s', [8, 1024], BF16)
        Ctab_s = sb('Ctab_s', [128, 2048], F32)
        KAR2 = Ctab_s.bitcast(BF16)
        bqkq_s = sb('bqkq_s', [128, 8], F32)
        bqkk_s = sb('bqkk_s', [128, 8], F32)
        bvp = sb('bvp', [1, 256], BF16)
        bg_s = sb('bg_s', [128, H], F32)
        bo_s = sb('bo_s', [128, 8], F32)
        fqb_s = sb('fqb_s', [128, 8], F32)
        fkb_s = sb('fkb_s', [128, 8], F32)
        fqbs_s = sb('fqbs_s', [128, 8], F32)
        fkbs_s = sb('fkbs_s', [128, 8], F32)

        fob_s = sb('fob_s', [128, 8], F32)
        fc_s = sb('fc_s', [128, 4], F32)
        n1w_s = sb('n1w_s', [128, 8], F32)
        n2w_s = sb('n2w_s', [128, 8], F32)
        cols = sb('cols', [128, 16], F32)
        rows = [sb(f'row{i}', [1, 512], F32) for i in range(4)]
        krow = sb('krow', [1, 512], BF16)
        qrow = krow
        ksct = sb('ksct', [128, 512], BF16)
        qsct = sb('qsct', [128, 512], BF16)
        BT2 = sb('bt2c', [128, 2048], BF16)
        bt = [BT2[:, 512 * i:512 * i + 512] for i in range(4)] \
            + [sb('bt4', [128, 512], BF16)]
        bt5 = sb('bt5', [128, 512], BF16)
        BTR = lambda i: RD('bt2c', 0, 128, 512 * i, 512 * i + 512)
        cpT = sb('cpT', [8, 2048], BF16)
        lgt = sb('lgt', [128, 8], F32)
        m1c = sb('m1c', [128, 1], F32)
        m2c = sb('m2c', [128, 1], F32)
        i1c = sb('i1c', [128, 8], F32)
        i2c = sb('i2c', [128, 8], F32)
        t8c = sb('t8c', [128, 8], F32)
        combc = sb('combc', [128, 8], F32)
        PSA = st.enter_context(nc.psum_tensor('psa', [128, 2048], F32))
        PSV = st.enter_context(nc.psum_tensor('psv', [128, 1024], F32))
        PSP = st.enter_context(nc.psum_tensor('psp', [128, 1024], F32))
        ps = [PSA[:, 512 * i:512 * i + 512] for i in range(4)] \
            + [PSP[:, 0:512], PSP[:, 512:1024]]
        ps7 = PSV[:, 0:512]
        _PSN = ['psa', 'psa', 'psa', 'psa', 'psp', 'psp']
        _PSO = [0, 512, 1024, 1536, 0, 512]
        def PSR(i):
            if i == 7:
                return RD('psv', 0, 128, 0, 512)
            return RD(_PSN[i], 0, 128, _PSO[i], _PSO[i] + 512)
        PS7R = PSR(7)
        CST = [RD('cst', 0, 1, 0, 1)]

        def cload(dst, src, cast=False):
            P.dma('io' if cast else 'sp',
                  lambda eng, d=dst, s=src: eng.dma_start(out=d, in_=s),
                  'cst', writes=[RD('cst', 0, 1, 0, 1)], tag='cst')
        for dst, src, cast in (
                (idf_s[:], idf, 0), (idb_s[:], idb, 0), (onescol_s[:], onescol, 0),
                (ones1b_s[:], ones1b, 0), (ones1f_s[:], ones1f, 0),
                (onehots_s[:], onehots_d, 0),
                (Ctab_s[:], Ctab_d, 0), (bqkq_s[:], bqkq, 0), (bqkk_s[:], bqkk, 0),
                (bg_s[:], bg, 0), (bo_s[:], bo, 0),
                (fqb_s[:], fqb, 0), (fkb_s[:], fkb, 0),
                (fqbs_s[:], fqbs, 0), (fkbs_s[:], fkbs, 0),
                (fob_s[:], fob, 0), (n1w_s[:], n1w, 0), (n2w_s[:], n2w, 0),
                (fc_s[:], fconst, 0)):
            cload(dst, src, cast)
        for e in range(E):
            P.dma('io', lambda eng, d=w3bf[2048 * e:2048 * e + 2048, :], s=w3T[e]:
                  eng.dma_start(out=d, in_=s), 'w3c',
                  writes=[RD('w3bf', 0, 128, 2048 * e, 2048 * e + 2048)], tag='w3c')

        for tau in range(32):
            pb = tau % 2
            xb = FAR[:, 1024 * pb:1024 * pb + 1024]
            xbr = RD('far', 0, 128, 1024 * pb, 1024 * pb + 1024)
            xnf = FAR[:, 2048 + 1024 * pb:3072 + 1024 * pb]
            xnfr = RD('far', 0, 128, 2048 + 1024 * pb, 3072 + 1024 * pb)
            sq = ZAR[:, 1024 * pb:1024 * pb + 1024]
            sqr = RD('zar', 0, 128, 1024 * pb, 1024 * pb + 1024)
            xnb = ZAR[:, 2048 + 1024 * pb:3072 + 1024 * pb]
            xnbr = RD('zar', 0, 128, 2048 + 1024 * pb, 3072 + 1024 * pb)
            xst = QAR[:, 1024 * pb:1024 * pb + 1024]
            xstr = RD('qar', 0, 128, 1024 * pb, 1024 * pb + 1024)
            c0 = 8 * pb
            def col(i, c0=c0):
                return (cols[:, c0 + i:c0 + i + 1],
                        RD('cols', 0, 128, c0 + i, c0 + i + 1))
            P.dma('sp', lambda eng, d=xb, s=xp[128 * tau:128 * tau + 128, :]:
                  eng.dma_start(out=d, in_=s), f'xin{pb}', writes=[xbr], tag='xA')
            s1, s1r = col(0)
            P.op('dve', lambda eng, o=s1, i=xb: eng.reduce_sum(o, i, AX.X),
                 reads=[xbr], writes=[s1r])
            s2, s2r = col(1)
            P.op('act', lambda eng, o=sq, i=xb, a=s2:
                 eng.activation(o, i, AF.Square, bias=0.0, scale=1.0, accum_out=a),
                 reads=[xbr], writes=[sqr, s2r])
            tt, ttr = col(2)
            P.op('dve', lambda eng, o=tt, i=s1: eng.tensor_mul(o, i, i),
                 reads=[s1r], writes=[ttr])
            u, ur = col(3)
            P.op('dve', lambda eng, o=u, i=tt, j=s2:
                 eng.scalar_tensor_tensor(o, i, -1.0 / 1024, j, ALU.mult, ALU.add),
                 reads=[ttr, s2r], writes=[ur])
            sdv, sdr = col(4)
            P.op('act', lambda eng, o=sdv, i=u:
                 eng.activation(o, i, AF.Sqrt, bias=fc_s[:, 0:1], scale=1.0 / 1024),
                 reads=[ur], writes=[sdr])
            rstd, rstdr = col(5)
            P.op('dve', lambda eng, o=rstd, i=sdv: eng.reciprocal(o, i),
                 reads=[sdr], writes=[rstdr])
            negm, negmr = col(6)
            P.op('dve', lambda eng, o=negm, i=s1:
                 eng.tensor_scalar_mul(o, i, -1.0 / 1024),
                 reads=[s1r], writes=[negmr])
            P.op('dve', lambda eng, o=xnf, i=xb, a=negm, b=rstd:
                 eng.tensor_scalar(o, i, a, b, ALU.add, ALU.mult),
                 reads=[xbr, negmr, rstdr], writes=[xnfr])
            aa, aar = col(7)
            P.op('dve', lambda eng, o=aa, i=u: eng.tensor_scalar_add(o, i, 1024 * LN_EPS),
                 reads=[ur], writes=[aar])
            P.op('dve', lambda eng, o=tt, i=aa: eng.reciprocal(o, i),
                 reads=[aar], writes=[ttr])
            P.op('dve', lambda eng, o=aa, i=u, j=tt: eng.tensor_mul(o, i, j),
                 reads=[ur, ttr], writes=[aar])
            P.op('act', lambda eng, o=tt, i=aa:
                 eng.activation(o, i, AF.Sqrt, bias=fc_s[:, 1:2], scale=1.0),
                 reads=[aar], writes=[ttr])
            P.op('dve', lambda eng, o=aa, i=tt: eng.reciprocal(o, i),
                 reads=[ttr], writes=[aar])
            def _tb(i):
                if i < 4:
                    return (PSA[:, 1536 + 128 * i:1536 + 128 * i + 128],
                            RD('psa', 0, 128, 1536 + 128 * i, 1536 + 128 * i + 128))
                return (PSP[:, 128 * (i - 4):128 * (i - 4) + 128],
                        RD('psp', 0, 128, 128 * (i - 4), 128 * (i - 4) + 128))
            for i in range(8):
                tb, tbr = _tb(i)
                P.op('pe', lambda eng, o=tb, a=xnf[:, 128 * i:128 * i + 128],
                     idn=idf_s[:]: eng.transpose(o, a, idn),
                     reads=[xnfr] + CST, writes=[tbr])
            for i in range(8):
                tb, tbr = _tb(i)
                P.op('dve', lambda eng, o=AB[i][:, 128 * tau:128 * tau + 128],
                     s=tb: eng.tensor_copy(o, s),
                     reads=[tbr],
                     writes=[RD(f'ab{i}', 0, 128, 128 * tau, 128 * tau + 128)])
            P.op('pe', lambda eng, o=ps7[0:1, 0:128], a=aa, idn=idf_s[:]:
                 eng.transpose(o, a, idn), reads=[aar] + CST, writes=[PS7R])
            P.op('act', lambda eng, o=rows[3][0:1, 0:128], s=ps7[0:1, 0:128]:
                 eng.activation(o, s, AF.Identity, bias=0.0, scale=1.0),
                 reads=[PS7R], writes=[RD('row3', 0, 1, 0, 128)])
            P.op('pe', lambda eng, o=ps[5][:, 0:128], l=ones1f_s[:],
                 r=rows[3][0:1, 0:128]:
                 eng.matmul(o, l, r, start=True, stop=True),
                 reads=[RD('row3', 0, 1, 0, 128)] + CST, writes=[PSR(5)])
            for i in range(8):
                P.op('dve', lambda eng, o=xst[:, 128 * i:128 * i + 128],
                     a=AB[i][:, 128 * tau:128 * tau + 128],
                     w=n1w_s[:, i:i + 1], b=ps[5][:, 0:128]:
                     eng.scalar_tensor_tensor(o, a, w, b, ALU.mult, ALU.mult),
                     reads=[RD(f'ab{i}', 0, 128, 128 * tau, 128 * tau + 128),
                            PSR(5), RD('cst', 0, 1, 0, 1)],
                     writes=[xstr])
            P.dma('act', lambda eng, d=xnr1d[:, 128 * tau:128 * tau + 128]
                  .rearrange("(ic p) t -> p ic t", p=128),
                  s=xst.rearrange("p (ic t) -> p ic t", t=128):
                  eng.dma_start(out=d, in_=s), 'xnr1w', reads=[xstr],
                  writes=[RD('xnr1d', 0, 128, 128 * tau, 128 * tau + 128)])
            if (tau // 4) % 2 == 0:
                om = (tau // 8) * 4 + tau % 4
                stg = SCR[:, 1024 * pb:1024 * pb + 1024]
                stgr = RD('scr', 0, 128, 1024 * pb, 1024 * pb + 1024)
                for i in range(8):
                    bk, bn = (ps[3], 3) if i < 4 else (ps[4], 4)
                    fo = 128 * (i % 4)
                    P.op('pe', lambda eng, o=bk[:, fo:fo + 128],
                         a=xnf[:, 128 * i:128 * i + 128], idn=idf_s[:]:
                         eng.transpose(o, a, idn),
                         reads=[xnfr] + CST, writes=[PSR(bn)])
                for i in range(8):
                    bk, bn = (ps[3], 3) if i < 4 else (ps[4], 4)
                    fo = 128 * (i % 4)
                    P.op('act', lambda eng, o=stg[:, 128 * i:128 * i + 128],
                         s=bk[:, fo:fo + 128]:
                         eng.activation(o, s, AF.Identity, bias=0.0, scale=1.0),
                         reads=[PSR(bn)], writes=[stgr])
                P.dma('act', lambda eng, d=resd[:, 128 * om:128 * om + 128]
                      .rearrange("(ic p) t -> p ic t", p=128),
                      s=stg.rearrange("p (ic t) -> p ic t", t=128):
                      eng.dma_start(out=d, in_=s), 'resw', reads=[stgr],
                      writes=[RD('resd', 0, 128, 128 * om, 128 * om + 128)])

        if _PHASES >= 2:
            WARr = lambda a, b: RD('war', 0, 128, a, b)
            def wload_b(dst, srct, a, b):
                P.dma('io', lambda eng, d=dst, s=srct:
                      eng.dma_start(out=d, in_=s), 'wb',
                      writes=[WARr(a, b)], tag='wB')
            def loadw_early(pr):
                wload_b(WAR[:, 0:2048].rearrange("p (ic c) -> p ic c", c=256),
                        wqkp[pr], 0, 2048)
                wload_b(WAR[:, 2048:4096].rearrange("p (ic c) -> p ic c", c=256),
                        wvp[pr], 2048, 4096)
                P.dma('io', lambda eng, d=bvp[:],
                      s=bvrow[None, 256 * pr:256 * pr + 256]:
                      eng.dma_start(out=d, in_=s), 'bvp',
                      writes=[RD('bvp', 0, 1, 0, 256)])
            def loadw_late(pr):
                wload_b(WAR[:, 4096:6144].rearrange("p (ic c) -> p ic c", c=256),
                        wgp[pr], 4096, 6144)
                wload_b(WAR[:, 6144:8192].rearrange("p (ic c) -> p ic c", c=1024),
                        wop[pr], 6144, 8192)
            loadw_early(0)
            for pr in range(_B_PAIRS):
                h0 = 2 * pr
                loadw_late(pr)
                for hh in range(2):
                    P.dma('sp', lambda eng, d=krow[:], s=kscd[h0 + hh:h0 + hh + 1, :]:
                          eng.dma_start(out=d, in_=s), 'krow',
                          writes=[RD('krow', 0, 1, 0, 512)])
                    P.op('pe', lambda eng, o=ps[5][64 * hh:64 * hh + 64, :],
                         l=ones1b_s[0:1, 0:64], r=krow[:]:
                         eng.matmul(o, l, r, start=True, stop=True),
                         reads=[RD('krow', 0, 1, 0, 512)] + CST, writes=[PSR(5)])
                P.op('act', lambda eng, o=ksct[:], s=ps[5][:]:
                     eng.activation(o, s, AF.Identity, bias=0.0, scale=1.0),
                     reads=[PSR(5)], writes=[RD('ksct', 0, 128, 0, 512)])
                for stile in range(2):
                    f0 = 512 * stile
                    kb, kbn = (ps[1], 1) if stile % 2 == 0 else (ps[2], 2)
                    for ic in range(8):
                        P.op('pe', lambda eng, o=kb[:],
                             l=WAR[:, 256 * ic + 128:256 * ic + 256],
                             r=AB[ic][:, f0:f0 + 512], s=(ic == 0), t=(ic == 7):
                             eng.matmul(o, l, r, start=s, stop=t),
                             reads=[WARr(0, 2048), RD(f'ab{ic}', 0, 128, f0, f0 + 512)],
                             writes=[PSR(kbn)])
                    P.op('dve', lambda eng, o=KAR[:, f0:f0 + 512], i=kb[:],
                         b=bqkk_s[:, pr:pr + 1], k=ksct[:]:
                         eng.scalar_tensor_tensor(o, i, b, k, ALU.add, ALU.mult),
                         reads=[PSR(kbn), RD('ksct', 0, 128, 0, 512),
                                RD('cst', 0, 1, 0, 1)],
                         writes=[RD('kar', 0, 128, f0, f0 + 512)])
                for sc in range(32):
                    vb, vbr = (ps[0], PSR(0)) if sc % 2 == 0 else (ps7, PS7R)
                    for ic in range(8):
                        P.op('pe', lambda eng, o=vb[:, 0:256],
                             l=AB[ic][:, 128 * sc:128 * sc + 128],
                             r=WAR[:, 2048 + 256 * ic:2048 + 256 * ic + 256],
                             s=(ic == 0): eng.matmul(o, l, r, start=s, stop=False),
                             reads=[WARr(2048, 4096),
                                    RD(f'ab{ic}', 0, 128, 128 * sc, 128 * sc + 128)],
                             writes=[vbr])
                    P.op('pe', lambda eng, o=vb[:, 0:256], l=ones1b_s[:],
                         r=bvp[:]:
                         eng.matmul(o, l, r, start=False, stop=True),
                         reads=[RD('bvp', 0, 1, 0, 256)] + CST, writes=[vbr])
                    P.op('dve', lambda eng, o=VAR[:, 256 * sc:256 * sc + 256],
                         i=vb[:, 0:256]: eng.tensor_copy(o, i),
                         reads=[vbr],
                         writes=[RD('var', 0, 128, 256 * sc, 256 * sc + 256)])
                for j in range(1):
                    q0 = 1024 * j
                    for hh in range(2):
                        ri = (h0 + hh) * 4 + j
                        P.dma('sp', lambda eng, d=qrow[:], s=qscd[ri:ri + 1, :]:
                              eng.dma_start(out=d, in_=s), 'qrow',
                              writes=[RD('krow', 0, 1, 0, 512)])
                        P.op('pe', lambda eng, o=ps[5][64 * hh:64 * hh + 64, :],
                             l=ones1b_s[0:1, 0:64], r=qrow[:]:
                             eng.matmul(o, l, r, start=True, stop=True),
                             reads=[RD('krow', 0, 1, 0, 512)] + CST, writes=[PSR(5)])
                    P.op('act', lambda eng, o=qsct[:], s=ps[5][:]:
                         eng.activation(o, s, AF.Identity, bias=0.0, scale=1.0),
                         reads=[PSR(5)], writes=[RD('qsct', 0, 128, 0, 512)])
                    for ic in range(8):
                        P.op('pe', lambda eng, o=ps[1][:],
                             l=WAR[:, 256 * ic:256 * ic + 128],
                             r=AB[ic][:, q0:q0 + 512], s=(ic == 0), t=(ic == 7):
                             eng.matmul(o, l, r, start=s, stop=t),
                             reads=[WARr(0, 2048), RD(f'ab{ic}', 0, 128, q0, q0 + 512)],
                             writes=[PSR(1)])
                    P.op('dve', lambda eng, o=QAR[:, 512 * j:512 * j + 512],
                         i=ps[1][:], b=bqkq_s[:, pr:pr + 1], k=qsct[:]:
                         eng.scalar_tensor_tensor(o, i, b, k, ALU.add, ALU.mult),
                         reads=[PSR(1), RD('qsct', 0, 128, 0, 512),
                                RD('cst', 0, 1, 0, 1)],
                         writes=[RD('qar', 0, 128, 512 * j, 512 * j + 512)])
                PSV1 = PSV[:, 512:1024]
                PSV1R = RD('psv', 0, 128, 512, 1024)
                bfill = []
                bdone = [0]
                def bpump(n=1):
                    for _ in range(min(n, len(bfill))):
                        bfill.pop(0)()
                        bdone[0] += 1
                def bdrain(target):
                    while bdone[0] < target and bfill:
                        bfill.pop(0)()
                        bdone[0] += 1
                def q_kstile(stile):
                    f0 = 512 * stile
                    for ic0 in (0, 4):
                        def mmk(ic0=ic0, f0=f0):
                            for ic in range(ic0, ic0 + 4):
                                P.op('pe', lambda eng, o=PSV1,
                                     l=WAR[:, 256 * ic + 128:256 * ic + 256],
                                     r=AB[ic][:, f0:f0 + 512],
                                     s=(ic == 0), t=(ic == 7):
                                     eng.matmul(o, l, r, start=s, stop=t),
                                     reads=[WARr(0, 2048),
                                            RD(f'ab{ic}', 0, 128, f0, f0 + 512)],
                                     writes=[PSV1R])
                        bfill.append(mmk)
                    def evk(f0=f0):
                        P.op('dve', lambda eng, o=KAR[:, f0:f0 + 512], i=PSV1,
                             b=bqkk_s[:, pr:pr + 1], k=ksct[:]:
                             eng.scalar_tensor_tensor(o, i, b, k, ALU.add,
                                                      ALU.mult),
                             reads=[PSV1R, RD('ksct', 0, 128, 0, 512),
                                    RD('cst', 0, 1, 0, 1)],
                             writes=[RD('kar', 0, 128, f0, f0 + 512)])
                    bfill.append(evk)
                def q_vchunk(sc):
                    def mmva(sc=sc):
                        for ic in range(4):
                            P.op('pe', lambda eng, o=PSV1[:, 0:256],
                                 l=AB[ic][:, 128 * sc:128 * sc + 128],
                                 r=WAR[:, 2048 + 256 * ic:2048 + 256 * ic + 256],
                                 s=(ic == 0):
                                 eng.matmul(o, l, r, start=s, stop=False),
                                 reads=[WARr(2048, 4096),
                                        RD(f'ab{ic}', 0, 128, 128 * sc,
                                           128 * sc + 128)],
                                 writes=[PSV1R])
                    def mmvb(sc=sc):
                        for ic in range(4, 8):
                            P.op('pe', lambda eng, o=PSV1[:, 0:256],
                                 l=AB[ic][:, 128 * sc:128 * sc + 128],
                                 r=WAR[:, 2048 + 256 * ic:2048 + 256 * ic + 256]:
                                 eng.matmul(o, l, r, start=False, stop=False),
                                 reads=[WARr(2048, 4096),
                                        RD(f'ab{ic}', 0, 128, 128 * sc,
                                           128 * sc + 128)],
                                 writes=[PSV1R])
                        P.op('pe', lambda eng, o=PSV1[:, 0:256], l=ones1b_s[:],
                             r=bvp[:]:
                             eng.matmul(o, l, r, start=False, stop=True),
                             reads=[RD('bvp', 0, 1, 0, 256)] + CST,
                             writes=[PSV1R])
                    def evv(sc=sc):
                        P.op('dve', lambda eng, o=VAR[:, 256 * sc:256 * sc + 256],
                             i=PSV1[:, 0:256]: eng.tensor_copy(o, i),
                             reads=[PSV1R],
                             writes=[RD('var', 0, 128, 256 * sc, 256 * sc + 256)])
                    bfill.extend((mmva, mmvb, evv))
                def q_qslot(j):
                    q0 = 1024 * j
                    def qsb(j=j):
                        for hh in range(2):
                            ri = (h0 + hh) * 4 + j
                            P.dma('sp', lambda eng, d=qrow[:],
                                  s=qscd[ri:ri + 1, :]:
                                  eng.dma_start(out=d, in_=s), 'qrow',
                                  writes=[RD('krow', 0, 1, 0, 512)])
                            P.op('pe', lambda eng,
                                 o=PSV1[64 * hh:64 * hh + 64, :],
                                 l=ones1b_s[0:1, 0:64], r=qrow[:]:
                                 eng.matmul(o, l, r, start=True, stop=True),
                                 reads=[RD('krow', 0, 1, 0, 512)] + CST,
                                 writes=[PSV1R])
                    def qse():
                        P.op('act', lambda eng, o=qsct[:], s=PSV1:
                             eng.activation(o, s, AF.Identity, bias=0.0,
                                            scale=1.0),
                             reads=[PSV1R], writes=[RD('qsct', 0, 128, 0, 512)])
                    for ic0 in (0, 4):
                        def mmq(ic0=ic0, q0=q0):
                            for ic in range(ic0, ic0 + 4):
                                P.op('pe', lambda eng, o=PSV1,
                                     l=WAR[:, 256 * ic:256 * ic + 128],
                                     r=AB[ic][:, q0:q0 + 512],
                                     s=(ic == 0), t=(ic == 7):
                                     eng.matmul(o, l, r, start=s, stop=t),
                                     reads=[WARr(0, 2048),
                                            RD(f'ab{ic}', 0, 128, q0, q0 + 512)],
                                     writes=[PSV1R])
                        if ic0 == 0:
                            bfill.extend((qsb, qse, mmq))
                        else:
                            bfill.append(mmq)
                    def evq(j=j):
                        P.op('dve', lambda eng, o=QAR[:, 512 * j:512 * j + 512],
                             i=PSV1, b=bqkq_s[:, pr:pr + 1], k=qsct[:]:
                             eng.scalar_tensor_tensor(o, i, b, k, ALU.add,
                                                      ALU.mult),
                             reads=[PSV1R, RD('qsct', 0, 128, 0, 512),
                                    RD('cst', 0, 1, 0, 1)],
                             writes=[RD('qar', 0, 128, 512 * j,
                                        512 * j + 512)])
                    bfill.append(evq)
                q_kstile(2); q_kstile(3); q_qslot(1)
                q_kstile(4); q_kstile(5); q_qslot(2)
                q_kstile(6); q_kstile(7); q_qslot(3)
                BDL = {2: 11, 4: 22, 6: 33}
                pend = []
                def pop_pend(nmax):
                    for _ in range(min(nmax, len(pend))):
                        pend.pop(0)()
                prevu = [None]
                tail_hold = [None]
                def emit_av(c):
                    pj, phh, plim, pp = prevu[0]
                    P.op('pe', lambda eng, o=ps[2][:],
                         l=VAR[:, 256 * c + 128 * phh:256 * c + 128 * phh + 128],
                         r=ZAR[:, 512 * c:512 * c + 512], s=(c == 0),
                         t=(c == plim - 1):
                         eng.matmul(o, l, r, start=s, stop=t),
                         reads=[RD('var', 0, 128, 256 * c + 128 * phh,
                                   256 * c + 128 * phh + 128),
                                RD('zar', 0, 128, 512 * c, 512 * c + 512)],
                         writes=[PSR(2)])
                def finish_prev():
                    while pend:
                        pend.pop(0)()
                    P.op('dve', lambda eng, o=SCR[:, 512:1024], i=ps[2][:]:
                         eng.tensor_copy(o, i),
                         reads=[PSR(2)], writes=[RD('scr', 0, 128, 512, 1024)])
                    pend.extend(tail_hold[0]())
                unit_idx = [0]
                for j in range(4 if _B_STAGE >= 2 else 0):
                    limit = 8 * (j + 1)
                    P.dma('sp', lambda eng,
                          d=MAT[:].rearrange("p (lc t) -> p lc t", t=512),
                          s=mask8_d[j].rearrange("lc p t -> p lc t"):
                          eng.dma_start(out=d, in_=s), 'msk',
                          writes=[RD('mat', 0, 128, 0, 4096)], tag='mskB')
                    for hh in range(2):
                        h = h0 + hh
                        hb = 64 * hh
                        u_i = unit_idx[0]
                        bdrain(BDL.get(u_i, 0))
                        if pr + 1 < _B_PAIRS and u_i == 6:
                            wload_b(WAR[:, 0:2048]
                                    .rearrange("p (ic c) -> p ic c", c=256),
                                    wqkp[pr + 1], 0, 2048)
                        if pr + 1 < _B_PAIRS and u_i == 0:
                            wload_b(WAR[:, 2048:4096]
                                    .rearrange("p (ic c) -> p ic c", c=256),
                                    wvp[pr + 1], 2048, 4096)
                            P.dma('io', lambda eng, d=bvp[:],
                                  s=bvrow[None, 256 * (pr + 1):
                                          256 * (pr + 1) + 256]:
                                  eng.dma_start(out=d, in_=s), 'bvp',
                                  writes=[RD('bvp', 0, 1, 0, 256)])
                        par = unit_idx[0] % 2
                        unit_idx[0] += 1
                        dclv = SCR[0:1, 1024 + 512 * par:1536 + 512 * par]
                        dclr = RD('scr', 0, 1, 1024 + 512 * par, 1536 + 512 * par)
                        qr = RD('qar', hb, hb + 64, 512 * j, 512 * j + 512)
                        def ones_mm(c, limit=limit):
                            at = bt[c % 4]
                            atr = BTR(c % 4)
                            P.op('pe', lambda eng, o=ps[1][0:1, :], l=onescol_s[:],
                                 r=at[:], s=(c == 0), t=(c == limit - 1):
                                 eng.matmul(o, l, r, start=s, stop=t),
                                 reads=[atr] + CST, writes=[PSR(1)])
                        plim = prevu[0][2] if prevu[0] is not None else 0
                        for c in range(max(limit, plim)):
                            bpump(1)
                            if prevu[0] is not None and c < plim:
                                emit_av(c)
                            if c >= limit:
                                pop_pend(2)
                                continue
                            sb_, sbn = (ps[0], 0) if c % 2 == 0 else (ps7, 7)
                            sbr = PSR(sbn)
                            P.op('pe', lambda eng, o=sb_[:],
                                 l=KAR[hb:hb + 64, 128 * c:128 * c + 128],
                                 r=QAR[hb:hb + 64, 512 * j:512 * j + 512]:
                                 eng.matmul(o, l, r, start=True, stop=True),
                                 reads=[RD('kar', hb, hb + 64, 128 * c, 128 * c + 128),
                                        qr],
                                 writes=[sbr])
                            ci = h * 128 + j * 32 + c
                            Sw = RD('zar', 0, 128, 512 * c, 512 * c + 512)
                            if c < 8 * j:
                                P.op('dve', lambda eng,
                                     o=ZAR[:, 512 * c:512 * c + 512], i=sb_[:],
                                     sc=Ctab_s[:, ci:ci + 1]:
                                     eng.tensor_scalar(o, i, sc, None, ALU.mult),
                                     reads=[sbr, RD('Ctab_s', 0, 128, ci, ci + 1)],
                                     writes=[Sw])
                            else:
                                lc = c - 8 * j
                                P.op('dve', lambda eng,
                                     o=ZAR[:, 512 * c:512 * c + 512], i=sb_[:],
                                     sc=Ctab_s[:, ci:ci + 1],
                                     m=MAT[:, 512 * lc:512 * lc + 512]:
                                     eng.scalar_tensor_tensor(o, i, sc, m,
                                                              ALU.mult, ALU.mult),
                                     reads=[sbr, RD('Ctab_s', 0, 128, ci, ci + 1),
                                            RD('mat', 0, 128, 512 * lc, 512 * lc + 512)],
                                     writes=[Sw])
                            if c % 2 == 1:
                                a0 = 512 * ((c - 1) % 4)
                                P.op('act', lambda eng, o=BT2[:, a0:a0 + 1024],
                                     i=ZAR[:, 512 * c - 512:512 * c + 512]:
                                     eng.activation(o, i, AF.Abs, bias=0.0,
                                                    scale=1.0),
                                     reads=[RD('zar', 0, 128, 512 * c - 512,
                                               512 * c + 512)],
                                     writes=[RD('bt2c', 0, 128, a0, a0 + 1024)])
                            if c >= 3:
                                ones_mm(c - 3)
                            pop_pend(2)
                        if prevu[0] is not None:
                            finish_prev()
                        ones_mm(limit - 3)
                        ones_mm(limit - 2)
                        ones_mm(limit - 1)
                        P.op('dve', lambda eng, o=dclv, i=ps[1][0:1, :]:
                             eng.tensor_scalar(o, i, 1.0, 50000.0, ALU.max, ALU.min),
                             reads=[PSR(1)], writes=[dclr])
                        prevu[0] = (j, hh, limit, par)
                        def tail(j=j, hh=hh, h=h, dclv=dclv, dclr=dclr):
                            T = []
                            T.append(lambda: P.op('act', lambda eng, o=bt[4][:],
                                 i=SCR[:, 512:1024]:
                                 eng.activation(o, i, AF.Square, bias=0.0, scale=1.0),
                                 reads=[RD('scr', 0, 128, 512, 1024)],
                                 writes=[RD('bt4', 0, 128, 0, 512)]))
                            T.append(lambda: P.op('pe', lambda eng, o=ps[3][0:1, :],
                                 l=onescol_s[:], r=bt[4][:]:
                                 eng.matmul(o, l, r, start=True, stop=True),
                                 reads=[RD('bt4', 0, 128, 0, 512)] + CST,
                                 writes=[PSR(3)]))
                            for ic in range(8):
                                T.append(lambda ic=ic: P.op('pe',
                                     lambda eng, o=ps[5][:],
                                     l=WAR[:, 4096 + 256 * ic + 128 * hh:
                                            4096 + 256 * ic + 128 * hh + 128],
                                     r=AB[ic][:, 1024 * j:1024 * j + 512],
                                     s=(ic == 0), t=(ic == 7):
                                     eng.matmul(o, l, r, start=s, stop=t),
                                     reads=[WARr(4096, 6144),
                                            RD(f'ab{ic}', 0, 128, 1024 * j,
                                               1024 * j + 512)],
                                     writes=[PSR(5)]))
                            T.append(lambda: P.op('dve', lambda eng, o=rows[1][:],
                                 i=dclv, j2=dclv:
                                 eng.scalar_tensor_tensor(o, i, EPS_RET, j2,
                                                          ALU.mult, ALU.mult),
                                 reads=[dclr],
                                 writes=[RD('row1', 0, 1, 0, 512)]))
                            T.append(lambda: P.op('dve', lambda eng, o=rows[2][:],
                                 i=ps[3][0:1, :], j2=rows[1][:]:
                                 eng.scalar_tensor_tensor(o, i, 1.0 / 128, j2,
                                                          ALU.mult, ALU.add),
                                 reads=[PSR(3), RD('row1', 0, 1, 0, 512)],
                                 writes=[RD('row2', 0, 1, 0, 512)]))
                            T.append(lambda: P.op('act', lambda eng, o=rows[1][:],
                                 i=rows[2][:]:
                                 eng.activation(o, i, AF.Ln, bias=0.0, scale=1.0),
                                 reads=[RD('row2', 0, 1, 0, 512)],
                                 writes=[RD('row1', 0, 1, 0, 512)]))
                            T.append(lambda: P.op('act', lambda eng, o=rows[2][:],
                                 i=rows[1][:]:
                                 eng.activation(o, i, AF.Exp, bias=0.0, scale=-0.5),
                                 reads=[RD('row1', 0, 1, 0, 512)],
                                 writes=[RD('row2', 0, 1, 0, 512)]))
                            T.append(lambda: P.op('pe', lambda eng, o=ps[4][:],
                                 l=ones1f_s[:], r=rows[2][:]:
                                 eng.matmul(o, l, r, start=True, stop=True),
                                 reads=[RD('row2', 0, 1, 0, 512)] + CST,
                                 writes=[PSR(4)]))
                            T.append(lambda: P.op('act', lambda eng, o=SCR[:, 0:512],
                                 s=ps[4][:]:
                                 eng.activation(o, s, AF.Identity, bias=0.0, scale=1.0),
                                 reads=[PSR(4)], writes=[RD('scr', 0, 128, 0, 512)]))
                            T.append(lambda: P.op('dve', lambda eng, o=bt[4][:],
                                 i=SCR[:, 512:1024], j2=SCR[:, 0:512]:
                                 eng.tensor_mul(o, i, j2),
                                 reads=[RD('scr', 0, 128, 512, 1024),
                                        RD('scr', 0, 128, 0, 512)],
                                 writes=[RD('bt4', 0, 128, 0, 512)]))
                            T.append(lambda: P.op('dve', lambda eng, o=bt5[:],
                                 i=ps[5][:], b=bg_s[:, h:h + 1], y=bt[4][:]:
                                 eng.scalar_tensor_tensor(o, i, b, y, ALU.add, ALU.mult),
                                 reads=[PSR(5), RD('bt4', 0, 128, 0, 512),
                                        RD('cst', 0, 1, 0, 1)],
                                 writes=[RD('bt5', 0, 128, 0, 512)]))
                            for oc in range(8):
                                bk, bn = (ps[4], 4) if oc % 2 == 0 else (ps[3], 3)
                                T.append(lambda oc=oc, bk=bk, bn=bn: P.op('pe',
                                     lambda eng, o=bk[:],
                                     l=WAR[:, 6144 + 1024 * hh + 128 * oc:
                                            6144 + 1024 * hh + 128 * oc + 128],
                                     r=bt5[:]: eng.matmul(o, l, r, start=True, stop=True),
                                     reads=[WARr(6144, 8192),
                                            RD('bt5', 0, 128, 0, 512)],
                                     writes=[PSR(bn)]))
                                sw = RD('far', 0, 128, 512 * oc, 512 * oc + 512)
                                if hh == 0:
                                    T.append(lambda oc=oc, bk=bk, bn=bn, sw=sw:
                                         P.op('dve', lambda eng,
                                              o=FAR[:, 512 * oc:512 * oc + 512],
                                              i=bk[:]: eng.tensor_copy(o, i),
                                              reads=[PSR(bn)], writes=[sw]))
                                else:
                                    T.append(lambda oc=oc, bk=bk, bn=bn, sw=sw:
                                         P.op('dve', lambda eng,
                                              o=FAR[:, 512 * oc:512 * oc + 512],
                                              i=bk[:],
                                              j2=FAR[:, 512 * oc:512 * oc + 512]:
                                              eng.tensor_add(o, i, j2),
                                              reads=[PSR(bn), sw], writes=[sw]))
                            if hh == 1:
                                aop = ((ALU.bypass if pr == 0 else ALU.add)
                                       if _B_SACC else ALU.bypass)
                                T.append(lambda aop=aop: P.dma('io', lambda eng,
                                     d=sd[:, 512 * j:512 * j + 512]
                                     .rearrange("(ic p) t -> p ic t", p=128),
                                     s=FAR[:].rearrange("p (ic t) -> p ic t", t=512),
                                     o=aop: eng.dma_start(out=d, in_=s, accum_op=o),
                                     'sacc',
                                     reads=[RD('far', 0, 128, 0, 4096)],
                                     writes=[RD('sd', 0, 128, 512 * j, 512 * j + 512)],
                                     tag='sacc'))
                            return T
                        tail_hold[0] = tail
                if prevu[0] is not None:
                    plim = prevu[0][2]
                    for c in range(plim):
                        emit_av(c)
                    finish_prev()
                    while pend:
                        pend.pop(0)()
                    prevu[0] = None
                bdrain(1 << 30)

        if _PHASES >= 3:
            WARr = lambda a, b: RD('war', 0, 128, a, b)
            ZR = lambda a, b: RD('zar', 0, 128, a, b)
            for i in range(8):
                P.dma('sp' if i % 2 == 0 else 'io',
                      lambda eng, d=AB[i][:, 0:4096],
                      s=xnr1d[128 * i:128 * i + 128, :]:
                      eng.dma_start(out=d, in_=s), 'xn1l',
                      reads=[RD('xnr1d', 0, 128, 0, 4096)],
                      writes=[RD(f'ab{i}', 0, 128, 0, 4096)], tag='xn1l')
            RKC, RKS = 2048, 6144
            for dst0, srct, ln in ((RKC, rkc, 4096), (RKS, rks, 4096)):
                P.dma('io', lambda eng, d=ZAR[:, dst0:dst0 + ln], s=srct:
                      eng.dma_start(out=d, in_=s), 'ropl',
                      writes=[ZR(dst0, dst0 + ln)], tag='ropl')
            P.op('dve', lambda eng,
                 o=VAR[:, 0:8320].rearrange("p (c x) -> p c x", x=65)[:, :, 64:65]:
                 eng.memset(o, 1.0),
                 reads=(), writes=[RD('var', 0, 128, 0, 8320)])
            KBUF = [(KAR, 'kar'), (KAR2, 'Ctab_s')]
            QBUF = [(QAR, 'qar', 0), (ZAR, 'zar', 14336)]
            unit_ctr = [0]
            pend_fin = [None]
            filler = []
            def pump(n=1):
                for _ in range(min(n, len(filler))):
                    filler.pop(0)()
            def drain():
                while filler:
                    filler.pop(0)()

            def load_qkw(g):
                P.dma('io', lambda eng,
                      d=WAR[:, 0:8192].rearrange("p (s ic c) -> p s ic c",
                                                 s=4, c=256),
                      s=fqp[g]: eng.dma_start(out=d, in_=s), 'wc',
                      writes=[WARr(0, 8192)], tag='wC')
                P.dma('io', lambda eng,
                      d=ZAR[:, 12288:14336].rearrange("p (ic c) -> p ic c", c=256),
                      s=fksp[g]: eng.dma_start(out=d, in_=s), 'wc',
                      writes=[ZR(12288, 14336)], tag='wC')
            def load_ow(g):
                P.dma('io', lambda eng,
                      d=ZAR[:, 10240:12288].rearrange("p (ic c) -> p ic c", c=1024),
                      s=fop[g]: eng.dma_start(out=d, in_=s), 'wc',
                      writes=[ZR(10240, 12288)], tag='wC')

            def v_loop(g):
                P.dma('io', lambda eng, d=bvp[:],
                      s=fvb[None, 256 * g:256 * g + 256]:
                      eng.dma_start(out=d, in_=s), 'bvp',
                      writes=[RD('bvp', 0, 1, 0, 256)])
                for sc in range(32):
                    vb, vbr = (ps[0], PSR(0)) if sc % 2 == 0 else (ps7, PS7R)
                    for ic in range(8):
                        P.op('pe', lambda eng, o=vb[:, 0:256],
                             l=AB[ic][:, 128 * sc:128 * sc + 128],
                             r=WAR[:, 4096 + 256 * ic:4096 + 256 * ic + 256],
                             s=(ic == 0): eng.matmul(o, l, r, start=s, stop=False),
                             reads=[WARr(4096, 6144),
                                    RD(f'ab{ic}', 0, 128, 128 * sc, 128 * sc + 128)],
                             writes=[vbr])
                    P.op('pe', lambda eng, o=vb[:, 0:256], l=ones1b_s[:],
                         r=bvp[:]:
                         eng.matmul(o, l, r, start=False, stop=True),
                         reads=[RD('bvp', 0, 1, 0, 256)] + CST, writes=[vbr])
                    for hg in range(4):
                        P.op('dve', lambda eng,
                             o=VAR[:, 260 * sc + 65 * hg:260 * sc + 65 * hg + 64],
                             i=vb[:, 64 * hg:64 * hg + 64]:
                             eng.tensor_copy(o, i),
                             reads=[vbr],
                             writes=[RD('var', 0, 128, 260 * sc + 65 * hg,
                                        260 * sc + 65 * hg + 64)])

            def _targets(g, duo, KB, QB):
                kt, kn = KB
                qt, qn, qoff = QB
                tg = []
                for stile in range(8):
                    tg.append((512 * stile, 2048, (ZAR, 'zar', 12288),
                               fkb_s, fkbs_s, RKC + 512 * stile, RKS + 512 * stile,
                               kt, kn, 512 * stile))
                for j in range(4):
                    tg.append((1024 * j, 0, (WAR, 'war', 6144),
                               fqb_s, fqbs_s, RKC + 1024 * j, RKS + 1024 * j,
                               qt, qn, qoff + 512 * j))
                return tg

            def proj_serial(g, duo, KB, QB):
                dbias = 2 * g + duo
                for idx, (tok0, wA0, (wBt, wBn, wB0), bA, bB, c0, s0,
                          dt_, dn, f0) in enumerate(_targets(g, duo, KB, QB)):
                    pA, pAn = (ps[0], 0) if idx % 2 == 0 else (ps[2], 2)
                    pB, pBn = (ps[1], 1) if idx % 2 == 0 else (ps[3], 3)
                    for pb, pbn, wt, wn, w0 in ((pA, pAn, WAR, 'war', wA0),
                                                (pB, pBn, wBt, wBn, wB0)):
                        wr = RD(wn, 0, 128, w0, w0 + 2048)
                        for ic in range(8):
                            P.op('pe', lambda eng, o=pb[:],
                                 l=wt[:, w0 + 256 * ic + 128 * duo:
                                       w0 + 256 * ic + 128 * duo + 128],
                                 r=AB[ic][:, tok0:tok0 + 512],
                                 s=(ic == 0), t=(ic == 7):
                                 eng.matmul(o, l, r, start=s, stop=t),
                                 reads=[wr,
                                        RD(f'ab{ic}', 0, 128, tok0, tok0 + 512)],
                                 writes=[PSR(pbn)])
                    sc0 = 1024 * (idx % 2)
                    P.op('dve', lambda eng, o=SCR[:, sc0:sc0 + 512], i=pA[:],
                         b=bA[:, dbias:dbias + 1], t=ZAR[:, c0:c0 + 512]:
                         eng.scalar_tensor_tensor(o, i, b, t, ALU.add, ALU.mult),
                         reads=[PSR(pAn), ZR(c0, c0 + 512),
                                RD('cst', 0, 1, 0, 1)],
                         writes=[RD('scr', 0, 128, sc0, sc0 + 512)])
                    P.op('dve', lambda eng, o=SCR[:, sc0 + 512:sc0 + 1024],
                         i=pB[:], b=bB[:, dbias:dbias + 1],
                         t=ZAR[:, s0:s0 + 512]:
                         eng.scalar_tensor_tensor(o, i, b, t, ALU.add, ALU.mult),
                         reads=[PSR(pBn), ZR(s0, s0 + 512),
                                RD('cst', 0, 1, 0, 1)],
                         writes=[RD('scr', 0, 128, sc0 + 512, sc0 + 1024)])
                    P.op('dve', lambda eng, o=dt_[:, f0:f0 + 512],
                         i=SCR[:, sc0:sc0 + 512],
                         j2=SCR[:, sc0 + 512:sc0 + 1024]:
                         eng.tensor_add(o, i, j2),
                         reads=[RD('scr', 0, 128, sc0, sc0 + 1024)],
                         writes=[RD(dn, 0, 128, f0, f0 + 512)])

            def queue_proj(g, duo, KB, QB):
                dbias = 2 * g + duo
                for (tok0, wA0, (wBt, wBn, wB0), bA, bB, c0, s0,
                     dt_, dn, f0) in _targets(g, duo, KB, QB):
                    for wt, wn, w0, bcol, tab0, scdst in (
                            (WAR, 'war', wA0, bA, c0, 512),
                            (wBt, wBn, wB0, bB, s0, 1024)):
                        for ic0 in (0, 4):
                            def mmi(wt=wt, wn=wn, w0=w0, ic0=ic0, tok0=tok0,
                                    duo=duo):
                                wr = RD(wn, 0, 128, w0, w0 + 2048)
                                for ic in range(ic0, ic0 + 4):
                                    P.op('pe', lambda eng, o=ps[5][:],
                                         l=wt[:, w0 + 256 * ic + 128 * duo:
                                               w0 + 256 * ic + 128 * duo + 128],
                                         r=AB[ic][:, tok0:tok0 + 512],
                                         s=(ic == 0), t=(ic == 7):
                                         eng.matmul(o, l, r, start=s, stop=t),
                                         reads=[wr, RD(f'ab{ic}', 0, 128,
                                                       tok0, tok0 + 512)],
                                         writes=[PSR(5)])
                            filler.append(mmi)
                        def evi(bcol=bcol, tab0=tab0, scdst=scdst, dbias=dbias):
                            P.op('dve', lambda eng,
                                 o=SCR[:, scdst:scdst + 512], i=ps[5][:],
                                 b=bcol[:, dbias:dbias + 1],
                                 t=ZAR[:, tab0:tab0 + 512]:
                                 eng.scalar_tensor_tensor(o, i, b, t,
                                                          ALU.add, ALU.mult),
                                 reads=[PSR(5), ZR(tab0, tab0 + 512),
                                        RD('cst', 0, 1, 0, 1)],
                                 writes=[RD('scr', 0, 128, scdst, scdst + 512)])
                        filler.append(evi)
                    def addi(dt_=dt_, dn=dn, f0=f0):
                        P.op('dve', lambda eng, o=dt_[:, f0:f0 + 512],
                             i=SCR[:, 512:1024], j2=SCR[:, 1024:1536]:
                             eng.tensor_add(o, i, j2),
                             reads=[RD('scr', 0, 128, 512, 1536)],
                             writes=[RD(dn, 0, 128, f0, f0 + 512)])
                    filler.append(addi)

            def emit_fin(hg, j, par):
                avs = PSV[:, 512 * par:512 * par + 512]
                avrow = RD('psv', 64, 65, 512 * par, 512 * par + 512)
                avlow = RD('psv', 0, 64, 512 * par, 512 * par + 512)
                ao = 64 * hg % 128
                af = 2048 * (hg // 2) + 512 * j
                def fin():
                    P.op('dve', lambda eng, o=rows[0][:],
                         i=avs[64:65, :]: eng.reciprocal(o, i),
                         reads=[avrow], writes=[RD('row0', 0, 1, 0, 512)])
                    P.op('pe', lambda eng, o=ps[4][0:64, :],
                         l=ones1f_s[0:1, 0:64], r=rows[0][:]:
                         eng.matmul(o, l, r, start=True, stop=True),
                         reads=[RD('row0', 0, 1, 0, 512)] + CST,
                         writes=[PSR(4)])
                    P.op('dve', lambda eng, o=SCR[0:64, 0:512],
                         i=ps[4][0:64, :]: eng.tensor_copy(o, i),
                         reads=[PSR(4)], writes=[RD('scr', 0, 64, 0, 512)])
                    P.op('dve', lambda eng, o=MAT[ao:ao + 64, af:af + 512],
                         i=avs[0:64, :], j2=SCR[0:64, 0:512]:
                         eng.tensor_mul(o, i, j2),
                         reads=[avlow, RD('scr', 0, 64, 0, 512)],
                         writes=[RD('mat', ao, ao + 64, af, af + 512)])
                return fin

            def units(g, duo, KB, QB):
                kt, kn = KB
                qt, qn, qoff = QB
                tot_items = len(filler)
                slot = [0]
                done = [0]
                def paced_pump():
                    slot[0] += 1
                    want = tot_items * slot[0] // 128
                    while done[0] < want and filler:
                        filler.pop(0)()
                        done[0] += 1
                for hh in range(2):
                    hb = 64 * hh
                    hg = 2 * duo + hh
                    for j in range(4):
                        par = unit_ctr[0] % 2
                        unit_ctr[0] += 1
                        avs = PSV[:, 512 * par:512 * par + 512]
                        avr = RD('psv', 0, 128, 512 * par, 512 * par + 512)
                        qr = RD(qn, hb, hb + 64, qoff + 512 * j,
                                qoff + 512 * j + 512)
                        def sc_mm(c, hb=hb, j=j, qr=qr):
                            sl = PSA[:, 512 * (c % 4):512 * (c % 4) + 512]
                            slr = RD('psa', 0, 128, 512 * (c % 4),
                                     512 * (c % 4) + 512)
                            P.op('pe', lambda eng, o=sl,
                                 l=kt[hb:hb + 64, 128 * c:128 * c + 128],
                                 r=qt[hb:hb + 64, qoff + 512 * j:
                                      qoff + 512 * j + 512]:
                                 eng.matmul(o, l, r, start=True, stop=True),
                                 reads=[RD(kn, hb, hb + 64, 128 * c,
                                           128 * c + 128), qr],
                                 writes=[slr])
                        def av_mm(c, hg=hg, avs=avs, avr=avr):
                            col = 1024 * ((c // 2) % 2) + 512 * (c % 2)
                            P.op('pe', lambda eng, o=avs[0:65, :],
                                 l=VAR[:, 260 * c + 65 * hg:
                                        260 * c + 65 * hg + 65],
                                 r=ZAR[:, col:col + 512],
                                 s=(c == 0), t=(c == 31):
                                 eng.matmul(o, l, r, start=s, stop=t),
                                 reads=[RD('var', 0, 128, 260 * c + 65 * hg,
                                           260 * c + 65 * hg + 65),
                                        ZR(col, col + 512)],
                                 writes=[avr])
                        for gi in range(16):
                            sc_mm(2 * gi)
                            sc_mm(2 * gi + 1)
                            if gi >= 1:
                                av_mm(2 * gi - 2)
                                av_mm(2 * gi - 1)
                            if gi == 1 and pend_fin[0] is not None:
                                pend_fin[0]()
                                pend_fin[0] = None
                            paced_pump()
                            src0 = 1024 * (gi % 2)
                            P.op('act', lambda eng,
                                 o=ZAR[:, src0:src0 + 1024],
                                 i=PSA[:, src0:src0 + 1024]:
                                 eng.activation(o, i, AF.Exp, bias=0.0,
                                                scale=1.0),
                                 reads=[RD('psa', 0, 128, src0, src0 + 1024)],
                                 writes=[ZR(src0, src0 + 1024)])
                        av_mm(30)
                        av_mm(31)
                        pend_fin[0] = emit_fin(hg, j, par)

            def out_proj(g):
                for j in range(4):
                    if j == 3 and pend_fin[0] is not None:
                        pend_fin[0]()
                        pend_fin[0] = None
                    for oc in range(8):
                        ob_, obn = (ps[5], 5) if oc % 2 == 0 else (ps[1], 1)
                        for ic in range(2):
                            P.op('pe', lambda eng, o=ob_[:],
                                 l=ZAR[:, 10240 + 1024 * ic + 128 * oc:
                                        10240 + 1024 * ic + 128 * oc + 128],
                                 r=MAT[:, 2048 * ic + 512 * j:
                                        2048 * ic + 512 * j + 512],
                                 s=(ic == 0), t=(ic == 1):
                                 eng.matmul(o, l, r, start=s, stop=t),
                                 reads=[ZR(10240, 12288),
                                        RD('mat', 0, 128, 2048 * ic + 512 * j,
                                           2048 * ic + 512 * j + 512)],
                                 writes=[PSR(obn)])
                        P.op('dve', lambda eng,
                             o=FAR[:, 512 * oc:512 * oc + 512],
                             i=ob_[:]: eng.tensor_copy(o, i),
                             reads=[PSR(obn)],
                             writes=[RD('far', 0, 128, 512 * oc, 512 * oc + 512)])
                    P.dma('io', lambda eng, d=resd[:, 512 * j:512 * j + 512]
                          .rearrange("(ic p) t -> p ic t", p=128),
                          s=FAR[:].rearrange("p (ic t) -> p ic t", t=512):
                          eng.dma_start(out=d, in_=s, accum_op=ALU.add), 'x1acc',
                          reads=[RD('far', 0, 128, 0, 4096)],
                          writes=[RD('resd', 0, 128, 512 * j, 512 * j + 512)],
                          tag='x1acc')

            load_qkw(0)
            load_ow(0)
            v_loop(0)
            proj_serial(0, 0, KBUF[0], QBUF[0])
            for grp in range(4):
                queue_proj(grp, 1, KBUF[1], QBUF[1])
                units(grp, 0, KBUF[0], QBUF[0])
                drain()
                if grp < 3:
                    load_qkw(grp + 1)
                    queue_proj(grp + 1, 0, KBUF[0], QBUF[0])
                units(grp, 1, KBUF[1], QBUF[1])
                drain()
                out_proj(grp)
                if grp < 3:
                    load_ow(grp + 1)
                    v_loop(grp + 1)

        if _PHASES >= 4:
            P.dma('io', lambda eng, d=WAR[:, 4096:4160]
                  .rearrange("p (ic c) -> p ic c", c=8),
                  s=gwT.rearrange("(ic p) c -> p ic c", p=128):
                  eng.dma_start(out=d, in_=s), 'wd',
                  writes=[RD('war', 0, 128, 4096, 4160)])
            for tj in range(4):
                for hf in range(2):
                    pb = (2 * tj + hf) % 2
                    X1 = FAR[:, 2048 * pb:2048 * pb + 2048]
                    x1r = RD('far', 0, 128, 2048 * pb, 2048 * pb + 2048)
                    cols0 = 512 * tj + 256 * hf
                    P.dma('sp', lambda eng,
                          d=X1.rearrange("p (ic t) -> p ic t", t=256),
                          s=resd[:, cols0:cols0 + 256]
                          .rearrange("(ic p) t -> p ic t", p=128):
                          eng.dma_start(out=d, in_=s), f'x1l{pb}',
                          reads=[RD('resd', 0, 128, cols0, cols0 + 256)],
                          writes=[x1r], tag='x1l')
                    for ic in range(8):
                        P.op('act', lambda eng, o=bt[0][:, 0:256],
                             i=X1[:, 256 * ic:256 * ic + 256], b=fob_s[:, ic:ic + 1]:
                             eng.activation(o, i, AF.Square, bias=b, scale=1.0),
                             reads=[x1r, RD('cst', 0, 1, 0, 1)],
                             writes=[BTR(0)])
                        P.op('pe', lambda eng, o=ps[3][0:1, 0:256], l=onescol_s[:],
                             r=bt[0][:, 0:256], s=(ic == 0), t=(ic == 7):
                             eng.matmul(o, l, r, start=s, stop=t),
                             reads=[BTR(0)] + CST,
                             writes=[PSR(3)])
                    P.op('act', lambda eng, o=rows[0][0:1, 0:256],
                         i=ps[3][0:1, 0:256]:
                         eng.activation(o, i, AF.Ln, bias=fc_s[0:1, 1:2], scale=1.0 / 1024),
                         reads=[PSR(3)], writes=[RD('row0', 0, 1, 0, 512)])
                    P.op('act', lambda eng, o=rows[1][0:1, 0:256],
                         i=rows[0][0:1, 0:256]:
                         eng.activation(o, i, AF.Exp, bias=0.0, scale=-0.5),
                         reads=[RD('row0', 0, 1, 0, 512)],
                         writes=[RD('row1', 0, 1, 0, 512)])
                    P.op('pe', lambda eng, o=ps[4][:, 0:256], l=ones1f_s[:],
                         r=rows[1][0:1, 0:256]:
                         eng.matmul(o, l, r, start=True, stop=True),
                         reads=[RD('row1', 0, 1, 0, 512)] + CST, writes=[PSR(4)])
                    P.op('act', lambda eng, o=SCR[:, 512:768], s=ps[4][:, 0:256]:
                         eng.activation(o, s, AF.Identity, bias=0.0, scale=1.0),
                         reads=[PSR(4)], writes=[RD('scr', 0, 128, 512, 768)])
                    for ic in range(8):
                        dc = 2048 * ic + cols0
                        P.op('dve', lambda eng, o=bt[1][:, 0:256],
                             i=X1[:, 256 * ic:256 * ic + 256], b=fob_s[:, ic:ic + 1]:
                             eng.tensor_scalar(o, i, b, None, ALU.add),
                             reads=[x1r, RD('cst', 0, 1, 0, 1)],
                             writes=[BTR(1)])
                        P.op('dve', lambda eng, o=ZAR[:, dc:dc + 256],
                             i=bt[1][:, 0:256], w=n2w_s[:, ic:ic + 1],
                             r2=SCR[:, 512:768]:
                             eng.scalar_tensor_tensor(o, i, w, r2, ALU.mult, ALU.mult),
                             reads=[BTR(1),
                                    RD('scr', 0, 128, 512, 768),
                                    RD('cst', 0, 1, 0, 1)],
                             writes=[RD('zar', 0, 128, dc, dc + 256)])
            XN2 = lambda ic, a, b: RD('zar', 0, 128, 2048 * ic + a, 2048 * ic + b)
            for tt in range(16):
                for ic in range(8):
                    P.op('pe', lambda eng, o=ps[3][:, 0:8],
                         l=ZAR[:, 2048 * ic + 128 * tt:2048 * ic + 128 * tt + 128],
                         r=WAR[:, 4096 + 8 * ic:4096 + 8 * ic + 8],
                         s=(ic == 0), t=(ic == 7): eng.matmul(o, l, r, start=s, stop=t),
                         reads=[XN2(ic, 128 * tt, 128 * tt + 128),
                                RD('war', 0, 128, 4096, 4160)],
                         writes=[PSR(3)])
                _gate_top2(P, ps, ps7, lgt, m1c, m2c, i1c, i2c, t8c, combc,
                           cpT, idf_s, tt, CST)
            for e in range(E):
                P.dma('sp', lambda eng, d=SCR[:, 0:16], s=b1l[e]:
                      eng.dma_start(out=d, in_=s), 'bexp',
                      writes=[RD('scr', 0, 128, 0, 16)])
                P.dma('sp', lambda eng, d=SCR[:, 16:32], s=b2l[e]:
                      eng.dma_start(out=d, in_=s), 'bexp',
                      writes=[RD('scr', 0, 128, 16, 32)])
                P.dma('sp', lambda eng, d=SCR[:, 32:40], s=b3l[e]:
                      eng.dma_start(out=d, in_=s), 'bexp',
                      writes=[RD('scr', 0, 128, 32, 40)])
                for k in range(16):
                    pb = k % 2
                    w1o, w2o = 1024 * pb, 2048 + 1024 * pb
                    w1w = RD('war', 0, 128, w1o, w1o + 1024)
                    w2w = RD('war', 0, 128, w2o, w2o + 1024)
                    w1s = WAR[:, w1o:w1o + 1024]
                    w2s = WAR[:, w2o:w2o + 1024]
                    P.dma('io', lambda eng,
                          d=w1s.rearrange("p (ic c) -> p ic c", c=128),
                          s=w1p[e][k]:
                          eng.dma_start(out=d, in_=s), f'w1l{pb}', writes=[w1w], tag='w1l')
                    P.dma('io', lambda eng,
                          d=w2s.rearrange("p (ic c) -> p ic c", c=128),
                          s=w2p[e][k]:
                          eng.dma_start(out=d, in_=s), f'w2l{pb}', writes=[w2w], tag='w2l')
                    hcol = 2048 * (k % 2)
                    hab = k // 2
                    for cb2 in range(4):
                        fcc = 512 * cb2
                        ub, ubn = (ps[0], 0) if cb2 % 2 == 0 else (ps[2], 2)
                        vb2, vb2n = (ps[1], 1) if cb2 % 2 == 0 else (ps[4], 4)
                        bs, bsn = (bt[0], bt[1]) if cb2 % 2 == 0 else (bt[2], bt[3])
                        ob0 = 0 if cb2 % 2 == 0 else 1024
                        for ic in range(8):
                            P.op('pe', lambda eng, o=ub[:],
                                 l=w1s[:, 128 * ic:128 * ic + 128],
                                 r=ZAR[:, 2048 * ic + fcc:2048 * ic + fcc + 512],
                                 s=(ic == 0), t=(ic == 7):
                                 eng.matmul(o, l, r, start=s, stop=t),
                                 reads=[w1w, XN2(ic, fcc, fcc + 512)],
                                 writes=[PSR(ubn)])
                        for ic in range(8):
                            P.op('pe', lambda eng, o=vb2[:],
                                 l=w2s[:, 128 * ic:128 * ic + 128],
                                 r=ZAR[:, 2048 * ic + fcc:2048 * ic + fcc + 512],
                                 s=(ic == 0), t=(ic == 7):
                                 eng.matmul(o, l, r, start=s, stop=t),
                                 reads=[w2w, XN2(ic, fcc, fcc + 512)],
                                 writes=[PSR(vb2n)])
                        P.op('act', lambda eng, o=bs[:], i=ub[:],
                             b=SCR[:, 0 + k:0 + k + 1]:
                             eng.activation(o, i, AF.Silu, bias=b, scale=1.0),
                             reads=[PSR(ubn), RD('scr', 0, 128, 0, 16)],
                             writes=[RD('bt2c', 0, 128, ob0, ob0 + 512)])
                        P.op('dve', lambda eng, o=bsn[:], i=vb2[:],
                             b=SCR[:, 16 + k:16 + k + 1]:
                             eng.tensor_scalar(o, i, b, None, ALU.add),
                             reads=[PSR(vb2n), RD('scr', 0, 128, 16, 32)],
                             writes=[RD('bt2c', 0, 128, ob0 + 512, ob0 + 1024)])
                        P.op('dve', lambda eng,
                             o=AB[hab][:, hcol + fcc:hcol + fcc + 512],
                             i=bs[:], j2=bsn[:]: eng.tensor_mul(o, i, j2),
                             reads=[RD('bt2c', 0, 128, ob0, ob0 + 1024)],
                             writes=[RD(f'ab{hab}', 0, 128, hcol + fcc,
                                        hcol + fcc + 512)])
                def w3slot(k):
                    if k < 8:
                        return VAR[:, 1024 * k:1024 * k + 1024], \
                            RD('var', 0, 128, 1024 * k, 1024 * k + 1024)
                    if k < 12:
                        return KAR[:, 1024 * (k - 8):1024 * (k - 8) + 1024], \
                            RD('kar', 0, 128, 1024 * (k - 8), 1024 * (k - 8) + 1024)
                    return MAT[:, 1024 * (k - 12):1024 * (k - 12) + 1024], \
                        RD('mat', 0, 128, 1024 * (k - 12), 1024 * (k - 12) + 1024)
                for k in range(16):
                    sl, slr = w3slot(k)
                    P.dma('sp', lambda eng, d=sl,
                          s=w3bf[2048 * e + 128 * k:2048 * e + 128 * k + 128, :]:
                          eng.dma_start(out=d, in_=s), 'w3l',
                          reads=[RD('w3bf', 0, 128, 2048 * e, 2048 * e + 2048)],
                          writes=[slr], tag='w3l')
                for tj in range(4):
                    P.op('pe', lambda eng, o=ps[5][:],
                         l=onehots_s[:, 128 * e:128 * e + 128],
                         r=cpT[0:8, 512 * tj:512 * tj + 512]:
                         eng.matmul(o, l, r, start=True, stop=True),
                         reads=[RD('cpT', 0, 8, 512 * tj, 512 * tj + 512)] + CST,
                         writes=[PSR(5)])
                    P.op('act', lambda eng, o=bt[4][:], s=ps[5][:]:
                         eng.activation(o, s, AF.Identity, bias=0.0, scale=1.0),
                         reads=[PSR(5)], writes=[RD('bt4', 0, 128, 0, 512)])
                    for ocg in range(2):
                        bks = ([(ps[0], 0), (ps[1], 1), (ps[2], 2), (ps[4], 4)]
                               if ocg == 0 else
                               [(ps[3], 3), (ps[5], 5), (ps7, 7), (ps[1], 1)])
                        for k in range(16):
                            sl, slr = w3slot(k)
                            for o4 in range(4):
                                oc = 4 * ocg + o4
                                bk, bn = bks[o4]
                                P.op('pe', lambda eng, o=bk[:],
                                     l=sl[:, 128 * oc:128 * oc + 128],
                                     r=AB[k // 2][:, 2048 * (k % 2) + 512 * tj:
                                                   2048 * (k % 2) + 512 * tj + 512],
                                     s=(k == 0), t=(k == 15):
                                     eng.matmul(o, l, r, start=s, stop=t),
                                     reads=[slr,
                                            RD(f'ab{k // 2}', 0, 128,
                                               2048 * (k % 2) + 512 * tj,
                                               2048 * (k % 2) + 512 * tj + 512)],
                                     writes=[PSR(bn)])
                        for o4 in range(4):
                            oc = 4 * ocg + o4
                            bk, bn = bks[o4]
                            P.op('dve', lambda eng,
                                 o=FAR[:, 512 * oc:512 * oc + 512], i=bk[:],
                                 b=SCR[:, 32 + oc:32 + oc + 1], cb=bt[4][:]:
                                 eng.scalar_tensor_tensor(o, i, b, cb,
                                                          ALU.add, ALU.mult),
                                 reads=[PSR(bn), RD('bt4', 0, 128, 0, 512),
                                        RD('scr', 0, 128, 32, 40)],
                                 writes=[RD('far', 0, 128, 512 * oc, 512 * oc + 512)])
                    P.dma('io', lambda eng, d=resd[:, 512 * tj:512 * tj + 512]
                          .rearrange("(ic p) t -> p ic t", p=128),
                          s=FAR[:].rearrange("p (ic t) -> p ic t", t=512):
                          eng.dma_start(out=d, in_=s, accum_op=ALU.add), 'x1acc',
                          reads=[RD('far', 0, 128, 0, 4096)],
                          writes=[RD('resd', 0, 128, 512 * tj, 512 * tj + 512)],
                          tag='yacc')

        if _PHASES >= 5:
            for om in range(16):
                j, sub = om // 4, om % 4
                pb = om % 2
                x2t = FAR[:, 0:1024]
                x2r = RD('far', 0, 128, 0, 1024)
                stt = FAR[:, 1024:2048]
                str_ = RD('far', 0, 128, 1024, 2048)
                xt = FAR[:, 2048:3072]
                xtr = RD('far', 0, 128, 2048, 3072)
                ot = FAR[:, 3072:4096]
                otr = RD('far', 0, 128, 3072, 4096)
                P.dma('sp', lambda eng, d=x2t.rearrange("p (ic t) -> p ic t", t=128),
                      s=resd[:, 128 * om:128 * om + 128]
                      .rearrange("(ic p) t -> p ic t", p=128):
                      eng.dma_start(out=d, in_=s), 'x2l',
                      reads=[RD('resd', 0, 128, 128 * om, 128 * om + 128)],
                      writes=[x2r], tag='x2l')
                P.dma('sp', lambda eng, d=stt.rearrange("p (ic t) -> p ic t", t=128),
                      s=sd[:, 128 * om:128 * om + 128]
                      .rearrange("(ic p) t -> p ic t", p=128):
                      eng.dma_start(out=d, in_=s), 'sl',
                      reads=[RD('sd', 0, 128, 128 * om, 128 * om + 128)],
                      writes=[str_], tag='sl')
                xrow = 1024 * j + 128 * sub
                P.dma('sp', lambda eng, d=xt, s=xp[xrow:xrow + 128, :]:
                      eng.dma_start(out=d, in_=s), 'xl2', writes=[xtr], tag='xl2')
                for ic in range(8):
                    P.op('act', lambda eng, o=SCR[:, 128 * ic:128 * ic + 128],
                         i=x2t[:, 128 * ic:128 * ic + 128], b=fob_s[:, ic:ic + 1]:
                         eng.activation(o, i, AF.Silu, bias=b, scale=1.0),
                         reads=[x2r, RD('cst', 0, 1, 0, 1)],
                         writes=[RD('scr', 0, 128, 128 * ic, 128 * ic + 128)])
                for ic in range(8):
                    P.op('dve', lambda eng, o=SCR[:, 1024 + 128 * ic:1152 + 128 * ic],
                         i=stt[:, 128 * ic:128 * ic + 128], b=bo_s[:, ic:ic + 1],
                         g2=SCR[:, 128 * ic:128 * ic + 128]:
                         eng.scalar_tensor_tensor(o, i, b, g2, ALU.add, ALU.mult),
                         reads=[str_, RD('scr', 0, 128, 128 * ic, 128 * ic + 128),
                                RD('cst', 0, 1, 0, 1)],
                         writes=[RD('scr', 0, 128, 1024 + 128 * ic, 1152 + 128 * ic)])
                for ic in range(8):
                    bk, bn = (ps[0], 0) if ic < 4 else (ps[1], 1)
                    fo = 128 * (ic % 4)
                    P.op('pe', lambda eng, o=bk[:, fo:fo + 128],
                         a=SCR[:, 1024 + 128 * ic:1152 + 128 * ic], idn=idf_s[:]:
                         eng.transpose(o, a, idn),
                         reads=[RD('scr', 0, 128, 1024 + 128 * ic, 1152 + 128 * ic)]
                         + CST,
                         writes=[PSR(bn)])
                for half in range(2):
                    bk, bn = (ps[0], 0) if half == 0 else (ps[1], 1)
                    P.op('dve', lambda eng, o=ot[:, 512 * half:512 * half + 512],
                         i=bk[:], j2=xt[:, 512 * half:512 * half + 512]:
                         eng.tensor_add(o, i, j2),
                         reads=[PSR(bn), xtr], writes=[otr])
                P.dma('act', lambda eng, d=out_d[128 * om:128 * om + 128, :], s=ot:
                      eng.dma_start(out=d, in_=s), 'out',
                      reads=[otr], writes=[RD('outd', 0, 128, 0, 1024)], tag='out')

        P.finalize(nc, st)
    return nc


def _gate_top2(P, ps, ps7, lgt, m1c, m2c, i1c, i2c, t8c, combc, cpT, idf_s, tt, CST):
    PSR3 = RD('ps3', 0, 128, 0, 512)
    PS7R = RD('ps7', 0, 128, 0, 512)
    P.op('dve', lambda eng, o=lgt[:], i=ps[3][:, 0:8]: eng.tensor_copy(o, i),
         reads=[PSR3], writes=[RD('lgt', 0, 128, 0, 8)])
    P.op('dve', lambda eng, o=m1c[:], i=lgt[:]: eng.reduce_max(o, i, AX.X),
         reads=[RD('lgt', 0, 128, 0, 8)], writes=[RD('m1c', 0, 128, 0, 1)])
    P.op('dve', lambda eng, o=i1c[:], i=lgt[:], m=m1c[:]:
         eng.tensor_scalar(o, i, m, None, ALU.is_ge),
         reads=[RD('lgt', 0, 128, 0, 8), RD('m1c', 0, 128, 0, 1)],
         writes=[RD('i1c', 0, 128, 0, 8)])
    P.op('dve', lambda eng, o=t8c[:], i=i1c[:], j2=lgt[:]:
         eng.scalar_tensor_tensor(o, i, -1e9, j2, ALU.mult, ALU.add),
         reads=[RD('i1c', 0, 128, 0, 8), RD('lgt', 0, 128, 0, 8)],
         writes=[RD('t8c', 0, 128, 0, 8)])
    P.op('dve', lambda eng, o=m2c[:], i=t8c[:]: eng.reduce_max(o, i, AX.X),
         reads=[RD('t8c', 0, 128, 0, 8)], writes=[RD('m2c', 0, 128, 0, 1)])
    P.op('dve', lambda eng, o=i2c[:], i=t8c[:], m=m2c[:]:
         eng.tensor_scalar(o, i, m, None, ALU.is_ge),
         reads=[RD('t8c', 0, 128, 0, 8), RD('m2c', 0, 128, 0, 1)],
         writes=[RD('i2c', 0, 128, 0, 8)])
    P.op('dve', lambda eng, o=m1c[:], i=m1c[:], j2=m2c[:]: eng.tensor_sub(o, i, j2),
         reads=[RD('m1c', 0, 128, 0, 1), RD('m2c', 0, 128, 0, 1)],
         writes=[RD('m1c', 0, 128, 0, 1)])
    P.op('act', lambda eng, o=m1c[:], i=m1c[:]:
         eng.activation(o, i, AF.Sigmoid, bias=0.0, scale=1.0),
         reads=[RD('m1c', 0, 128, 0, 1)], writes=[RD('m1c', 0, 128, 0, 1)])
    P.op('dve', lambda eng, o=combc[:], i=i1c[:], m=m1c[:]:
         eng.tensor_scalar(o, i, m, None, ALU.mult),
         reads=[RD('i1c', 0, 128, 0, 8), RD('m1c', 0, 128, 0, 1)],
         writes=[RD('combc', 0, 128, 0, 8)])
    P.op('dve', lambda eng, o=t8c[:], i=i2c[:], m=m1c[:]:
         eng.tensor_scalar(o, i, m, -1.0, ALU.mult, ALU.mult),
         reads=[RD('i2c', 0, 128, 0, 8), RD('m1c', 0, 128, 0, 1)],
         writes=[RD('t8c', 0, 128, 0, 8)])
    P.op('dve', lambda eng, o=combc[:], i=combc[:], j2=t8c[:]:
         eng.tensor_add(o, i, j2),
         reads=[RD('combc', 0, 128, 0, 8), RD('t8c', 0, 128, 0, 8)],
         writes=[RD('combc', 0, 128, 0, 8)])
    P.op('dve', lambda eng, o=combc[:], i=combc[:], j2=i2c[:]:
         eng.tensor_add(o, i, j2),
         reads=[RD('combc', 0, 128, 0, 8), RD('i2c', 0, 128, 0, 8)],
         writes=[RD('combc', 0, 128, 0, 8)])
    P.op('pe', lambda eng, o=ps7[0:8, 0:128], a=combc[:], idn=idf_s[:]:
         eng.transpose(o, a, idn),
         reads=[RD('combc', 0, 128, 0, 8)] + CST, writes=[PS7R])
    P.op('dve', lambda eng, o=cpT[0:8, 128 * tt:128 * tt + 128],
         i=ps7[0:8, 0:128]: eng.tensor_copy(o, i),
         reads=[PS7R], writes=[RD('cpT', 0, 8, 128 * tt, 128 * tt + 128)])



_CACHE = {}

def _get_nc():
    if 'nc' not in _CACHE:
        nc = bass.Bass("TRN2", target_bir_lowering=False)
        _build(nc)
        _CACHE['nc'] = nc
    return _CACHE['nc']


def _prep_shared(d):
    ac = np.ascontiguousarray
    Wqk = np.asarray(d['L0_attn_W_qk'])
    W3 = Wqk.reshape(8, 128, 2048)
    q4 = W3[:, :, :1024].reshape(8, 128, 8, 128).transpose(2, 1, 0, 3)
    k4 = W3[:, :, 1024:].reshape(8, 128, 8, 128).transpose(2, 1, 0, 3)
    qkvT = np.asarray(d['L0_ffn_qkv_w']).T
    def _swapcols(W):
        W4 = W.reshape(-1, 16, 2, 32)
        return np.stack([-W4[:, :, 1], W4[:, :, 0]], axis=2).reshape(-1, 1024)
    qkvT4 = np.concatenate([qkvT, _swapcols(qkvT[:, :1024])], axis=1)
    ksw = _swapcols(qkvT[:, 1024:2048])
    def _swaprows(Bm):
        B4 = Bm.reshape(2, 2, 32, 8)
        return np.ascontiguousarray(
            np.stack([-B4[:, 1], B4[:, 0]], axis=1).reshape(128, 8))
    outwT = np.asarray(d['L0_ffn_out_w']).T
    w1T = np.asarray(d['L0_ffn_w1']).transpose(0, 2, 1)
    w2T = np.asarray(d['L0_ffn_w2']).transpose(0, 2, 1)
    shared = dict(
        wqkp=ac(np.concatenate([q4, k4], axis=3)),
        wvp=ac(np.asarray(d['L0_attn_W_v']).reshape(8, 128, 8, 256).transpose(2, 1, 0, 3)),
        wgp=ac(np.asarray(d['L0_attn_W_g']).reshape(8, 128, 8, 256).transpose(2, 1, 0, 3)),
        wop=ac(np.asarray(d['L0_attn_W_o']).reshape(8, 2, 128, 1024).transpose(0, 2, 1, 3)),
        fqp=ac(qkvT4.reshape(8, 128, 4, 4, 256).transpose(3, 1, 2, 0, 4)),
        fksp=ac(ksw.reshape(8, 128, 4, 256).transpose(2, 1, 0, 3)),
        fop=ac(outwT.reshape(4, 2, 128, 1024).transpose(0, 2, 1, 3)),
        w1p=ac(w1T.reshape(E, 8, 128, 16, 128).transpose(0, 3, 2, 1, 4)),
        w2p=ac(w2T.reshape(E, 8, 128, 16, 128).transpose(0, 3, 2, 1, 4)),
        bqkq=ac(np.asarray(d['L0_attn_b_qk'])[:1024].reshape(8, 128).T),
        bqkk=ac(np.asarray(d['L0_attn_b_qk'])[1024:].reshape(8, 128).T),
        bvrow=ac(np.asarray(d['L0_attn_b_v'])),
        bg=ac(np.asarray(d['L0_attn_b_g']).reshape(16, 128).T),
        bo=ac(np.asarray(d['L0_attn_b_o']).reshape(8, 128).T),
        fqb=ac(np.asarray(d['L0_ffn_qkv_b'])[:1024].reshape(8, 128).T),
        fkb=ac(np.asarray(d['L0_ffn_qkv_b'])[1024:2048].reshape(8, 128).T),
        fqbs=_swaprows(np.asarray(d['L0_ffn_qkv_b'])[:1024].reshape(8, 128).T),
        fkbs=_swaprows(np.asarray(d['L0_ffn_qkv_b'])[1024:2048].reshape(8, 128).T),
        fvb=ac(np.asarray(d['L0_ffn_qkv_b'])[2048:]),
        fob=ac(np.asarray(d['L0_ffn_out_b']).reshape(8, 128).T),
        n1w=ac(np.asarray(d['L0_ffn_norm1_w']).reshape(8, 128).T),
        n2w=ac(np.asarray(d['L0_ffn_norm2_w']).reshape(8, 128).T),
        gwT=ac(np.asarray(d['L0_ffn_gate_w']).T),
        w3T=ac(np.asarray(d['L0_ffn_w3']).transpose(0, 2, 1)),
        b1l=ac(np.asarray(d['L0_ffn_b1']).reshape(E, 16, 128).transpose(0, 2, 1)),
        b2l=ac(np.asarray(d['L0_ffn_b2']).reshape(E, 16, 128).transpose(0, 2, 1)),
        b3l=ac(np.asarray(d['L0_ffn_b3']).reshape(E, 8, 128).transpose(0, 2, 1)),
    )
    shared.update(_consts())
    return shared


def _in_map(shared, tabs, x, core):
    b, half = core // 2, core % 2
    G = _gperm(half)
    m = dict(shared)
    m.update(tabs[half])
    m['xp'] = np.ascontiguousarray(x[b].reshape(8, 512, C)[G].reshape(T, C))
    return m


def kernel(**inputs):
    x = np.asarray(inputs['L0_attn_x'])
    shared = _prep_shared(inputs)
    tabs = [_host_tables(0), _host_tables(1)]
    in_maps = [_in_map(shared, tabs, x, c) for c in range(8)]
    nc = _get_nc()
    res = run_bass_kernel_spmd(nc, in_maps, core_ids=list(range(8)))
    out = np.zeros((B, T, C), np.float32)
    for c in range(8):
        b, half = c // 2, c % 2
        G = _gperm(half)
        oc = res.results[c]['out']
        for j in range(OWN):
            gt = G[2 * j]
            out[b, 512 * gt:512 * gt + 512] = oc[512 * j:512 * j + 512]
    return out
